# version 18
# baseline (speedup 1.0000x reference)
import math
import numpy as np
import ml_dtypes
import concourse.bass as bass
import concourse.mybir as mybir
from concourse.bass_utils import run_bass_kernel_spmd

F32 = mybir.dt.float32
BF16 = mybir.dt.bfloat16
F16 = mybir.dt.float16
AX = mybir.AxisListType
ALU = mybir.AluOpType
AF = mybir.ActivationFunctionType

NCORE = 8
SEQ = 8192
D = 2048
RPC = SEQ // NCORE
NL = 4
NFF = 8192
DC = D // 128
TJ = RPC // 128
NSUB = 16
NC2 = NFF // 128
SCALE = 1.0 / math.sqrt(D)
NEG_BIG = 3.0e38

ENGS = ("sync", "tensor", "vector", "scalar", "gpsimd")


class Sched:

    def __init__(self):
        self.items = {e: [] for e in ENGS}
        self.cnt = {}

    def c(self, sem):
        return self.cnt.get(sem, 0)

    def add(self, eng, fn, waits=(), inc=None, inck=1):
        val = None
        if inc is not None:
            self.cnt[inc] = self.cnt.get(inc, 0) + inck
            val = self.cnt[inc]
        ws = [(s, v) for (s, v) in waits if v and v > 0]
        self.items[eng].append((fn, ws, inc, inck))
        return val

    def emit(self, blk, sems):
        def mk(name):
            def run(eng):
                seen = {}
                for fn, waits, inc, inck in self.items[name]:
                    for (s, v) in waits:
                        if seen.get(s, 0) < v:
                            eng.wait_ge(sems[s], v)
                            seen[s] = v
                    inst = fn(eng) if fn is not None else None
                    if inc is not None:
                        if inst is not None:
                            inst.then_inc(sems[inc], inck)
                        else:
                            eng.sem_inc(sems[inc], inck)
            return run
        blk.sync(mk("sync"))
        blk.tensor(mk("tensor"))
        blk.vector(mk("vector"))
        blk.scalar(mk("scalar"))
        blk.gpsimd(mk("gpsimd"))


def build_nc():
    nc = bass.Bass("TRN2", target_bir_lowering=False)

    def ein(name, shape, dt):
        return nc.dram_tensor(name, shape, dt, kind="ExternalInput").ap()

    x_in = ein("x", [RPC, D], BF16)
    wqc = ein("wqc", [NL, DC, 128, DC, 128], BF16)
    wkc = ein("wkc", [NL, DC, 128, DC, 128], BF16)
    woc = ein("woc", [NL, DC, 128, DC, 128], BF16)
    wvn = ein("wvn", [NL, D, D], BF16)
    bqr = ein("bqr", [128, NL, DC], BF16)
    bkr = ein("bkr", [128, NL, DC], BF16)
    bor = ein("bor", [128, NL, DC], BF16)
    bvn = ein("bvn", [NL, D], BF16)
    w0s = ein("w0s", [RPC, D], F32)
    w1s = ein("w1s", [D, RPC], F32)
    b0r = ein("b0r", [128, NC2], F32)
    b1r = ein("b1r", [128, DC], F32)
    identb = ein("identb", [128, 128], BF16)
    identf = ein("identf", [128, 128], F32)
    onesb = ein("onesb", [1, 128], BF16)

    y_t = nc.dram_tensor("y_t", [D, RPC], F32, kind="ExternalOutput").ap()

    kv_in = nc.dram_tensor("kv_in", [4 * 128, SEQ], BF16).ap()
    g_k = [[nc.dram_tensor(f"g_k{s}{b}", [NCORE * 128, SEQ], BF16,
                           addr_space="Shared").ap()
            for b in range(2)] for s in range(2)]
    g_v = [[nc.dram_tensor(f"g_v{s}{b}", [NCORE * 128, SEQ], BF16,
                           addr_space="Shared").ap()
            for b in range(2)] for s in range(2)]
    w_in = nc.dram_tensor("w_in", [4 * 8 * 128, D], F16).ap()
    w_alls = [nc.dram_tensor(f"w_all{s}", [NCORE * 8 * 128, D], F16,
                             addr_space="Shared").ap()
              for s in range(4)]

    kv_in_v = kv_in.rearrange("(a p) n -> a p n", p=128)
    g_k_v = [[g_k[s][b].rearrange("(r p) n -> r p n", p=128) for b in range(2)] for s in range(2)]
    g_v_v = [[g_v[s][b].rearrange("(r p) n -> r p n", p=128) for b in range(2)] for s in range(2)]
    w_in_v = w_in.rearrange("(s l p) h -> s l p h", l=8, p=128)
    w_in_pv = w_in.rearrange("(s l p) h -> s p l h", l=8, p=128)
    w_all_v = [w_alls[s].rearrange("(r l p) h -> r l p h", l=8, p=128)
               for s in range(4)]

    def sb(name, shape, dt, off):
        return nc.alloc_sbuf_tensor_at(name, shape, dt, offset=off)

    BASE = 16512
    x0rm = sb("x0rm", [128, TJ, D], BF16, BASE)
    XT = sb("XT", [128, DC, RPC], F16, BASE)
    RA = BASE + 32768
    lnout = sb("lnout", [128, TJ, D], BF16, RA)
    kTst = sb("kTst", [128, DC, RPC], BF16, RA)
    d16b = sb("d16b", [128, 32, 512], F16, RA)
    oacc = sb("oacc", [128, TJ, D], F32, RA)
    xT = sb("xT", [128, DC, RPC], BF16, RA + 32768)
    RB = RA + 65536
    qT = sb("qT", [128, DC, RPC], BF16, RB)
    attnT = sb("attnT", [128, DC, RPC], BF16, RB)
    wch = sb("wch", [128, D], F32, RB)
    w0stg = sb("w0stg", [128, 2, DC, 128], F16, RB + 8192)
    w1stg = sb("w1stg", [128, 2, 8, 4, 128], F16, RB + 16384)
    wvhi8 = sb("wvhi8", [128, 8, D], BF16, RB)
    w0buf = sb("w0buf", [128, 2, 2, DC * 128], F16, RB)
    w1buf = sb("w1buf", [128, 8, 2, 512], F16, RB + 16384)
    KV = RB + 32768
    kslot = sb("kslot", [128, DC, 512], BF16, KV)
    vslot = sb("vslot", [128, 4, D], BF16, KV + 16384)
    wvlo8 = sb("wvlo8", [128, 8, D], BF16, KV)
    d16a = sb("d16a", [128, 32, 512], F16, KV)
    SM = KV + 32768
    projw = sb("projw", [128, 4, DC, 128], BF16, SM)
    pex = sb("pex", [128, TJ, 512], BF16, SM + 16384)
    e16s = sb("e16s", [128, 2, 512], F16, SM + 16384)
    ptsb = sb("ptsb", [128, 2, 4, 128], BF16, SM + 24576)
    attnj = sb("attnj", [128, 2, D], BF16, SM + 26624)
    vslot2 = sb("vslot2", [128, 4, D], BF16, SM)
    wch2 = sb("wch2", [128, D], F32, SM + 26624)
    outb = sb("outb", [128, 2, 512], F32, SM + 26624)
    bvb = sb("bvb", [1, D], BF16, SM + 34816)
    sbq = sb("sbq", [128, NL, DC], BF16, SM + 38912)
    sbk = sb("sbk", [128, NL, DC], BF16, SM + 39040)
    sbo = sb("sbo", [128, NL, DC], BF16, SM + 39168)
    sb0 = sb("sb0", [128, NC2], F32, SM + 39296)
    sb1 = sb("sb1", [128, DC], F32, SM + 39552)
    sidb = sb("sidb", [128, 128], BF16, SM + 39616)
    sidf = sb("sidf", [128, 128], F32, SM + 39872)
    sonb = sb("sonb", [1, 128], BF16, SM + 40384)
    ST = SM + 40640
    negm = sb("negm", [128, 2, TJ], F32, ST)
    nmraw = sb("nmraw", [128, TJ], F32, ST + 64)
    dcol = sb("dcol", [128, TJ], F32, ST + 96)
    alpha = sb("alpha", [128, TJ], F32, ST + 128)
    lblk = sb("lblk", [128, TJ], F32, ST + 160)
    lrun = sb("lrun", [128, TJ], F32, ST + 192)
    rlv = sb("rlv", [128, TJ], F32, ST + 224)
    lnsum = sb("lnsum", [128, TJ], F32, ST + 256)
    lnsq = sb("lnsq", [128, TJ], F32, ST + 288)
    lnm = sb("lnm", [128, TJ], F32, ST + 320)
    lnr = sb("lnr", [128, TJ], F32, ST + 352)
    sbf = sb("sbf", [128, 4, 512], BF16, ST + 512)
    assert ST + 512 + 4096 <= 229376

    psS = [nc.alloc_psum_tensor("psS0", [128, 512], F32),
           nc.alloc_psum_tensor("psS1", [128, 512], F32)]
    psTs = [nc.alloc_psum_tensor("psT0", [128, 512], BF16),
            nc.alloc_psum_tensor("psT1", [128, 512], BF16)]
    psPV = nc.alloc_psum_tensor("psPV", [128, 2048], F32)
    psTs_k = [t.ap().rearrange("p (k d) -> p k d", k=4) for t in psTs]

    S = Sched()
    c = S.c

    def dma(eng, sem, out, in_, waits=()):
        return S.add(eng, lambda e, o=out, i=in_: e.dma_start(out=o, in_=i),
                     waits=waits, inc=sem, inck=16)

    x_in_v = x_in.rearrange("(t p) d -> p t d", p=128)
    for h in range(2):
        dma(("sync", "scalar")[h], "LD_X%d" % h, x0rm[:, 4 * h:4 * h + 4, :],
            x_in_v[:, 4 * h:4 * h + 4, :])
    dma("sync", "LD_INIT", sbq[:, :, :], bqr)
    dma("sync", "LD_INIT", sbk[:, :, :], bkr)
    dma("sync", "LD_INIT", sbo[:, :, :], bor)
    dma("sync", "LD_INIT", sb0[:, :], b0r)
    dma("sync", "LD_INIT", sb1[:, :], b1r)
    dma("sync", "LD_INIT", sidb[:, :], identb)
    dma("sync", "LD_INIT", sidf[:, :], identf)
    dma("sync", "LD_INIT", sonb[:, :], onesb)
    ev_x0 = c("LD_INIT")

    S.add("gpsimd", lambda e: e.memset(negm[:, :, :], NEG_BIG))
    S.add("gpsimd", lambda e: e.memset(lrun[:, :], 0.0))
    S.add("gpsimd", lambda e: e.drain(), inc="GPS")

    def ln_row(t):
        return lnout[:, t, :]

    def emit_ln(src_row, wait_dve, wait_act):
        wd = wait_dve if isinstance(wait_dve, dict) else {0: wait_dve}
        wa = wait_act if isinstance(wait_act, dict) else {0: wait_act}
        for t in range(TJ):
            S.add("vector",
                  lambda e, t=t: e.reduce_sum(lnsum[:, t:t + 1], src_row(t), AX.X),
                  waits=wd.get(t, []), inc="DVE")
        ev_sums = c("DVE")
        for t in range(TJ):
            S.add("scalar",
                  lambda e, t=t: e.activation(ln_row(t), src_row(t), AF.Square,
                                              bias=0.0, scale=1.0,
                                              accum_out=lnsq[:, t:t + 1]),
                  waits=wa.get(t, []), inc="ACT")
        ev_sq = c("ACT")
        S.add("vector", lambda e: e.tensor_scalar(lnm[:, :], lnsum[:, :], 1.0 / D, None, ALU.mult),
              waits=[("DVE", ev_sums)])
        S.add("vector", lambda e: e.drain())
        S.add("vector", lambda e: e.tensor_scalar(lnr[:, :], lnsq[:, :], 1.0 / D, None, ALU.mult),
              waits=[("ACT", ev_sq)])
        S.add("vector", lambda e: e.drain())
        S.add("vector", lambda e: e.tensor_mul(lnsq[:, :], lnm[:, :], lnm[:, :]))
        S.add("vector", lambda e: e.drain())
        S.add("vector", lambda e: e.tensor_sub(lnr[:, :], lnr[:, :], lnsq[:, :]))
        S.add("vector", lambda e: e.drain())
        S.add("vector", lambda e: e.tensor_scalar(lnr[:, :], lnr[:, :], 1e-5, None, ALU.add))
        S.add("vector", lambda e: e.drain(), inc="DVE")
        ev_var = c("DVE")
        S.add("scalar", lambda e: e.activation(lnr[:, :], lnr[:, :], AF.Ln, bias=0.0, scale=1.0),
              waits=[("DVE", ev_var)])
        S.add("scalar", lambda e: e.drain())
        S.add("scalar", lambda e: e.activation(lnr[:, :], lnr[:, :], AF.Exp, bias=0.0, scale=-0.5))
        S.add("scalar", lambda e: e.drain(), inc="ACT")
        ev_rstd = c("ACT")
        for t in range(TJ):
            S.add("vector",
                  lambda e, t=t: e.tensor_scalar(ln_row(t), src_row(t),
                                                 lnm[:, t:t + 1], lnr[:, t:t + 1],
                                                 ALU.subtract, ALU.mult),
                  waits=[("ACT", ev_rstd)] if t == 0 else [], inc="DVE")
        return c("DVE")

    def emit_rm_to_ct(src_row, dst_ct, wait_dve_val, extra0=()):
        act0 = c("ACT")
        evs = []
        gidx = 0
        for dc in range(DC):
            for g in range(2):
                half = gidx % 2
                def tp(e, dc=dc, g=g, half=half):
                    last = None
                    for k in range(4):
                        t = g * 4 + k
                        last = e.transpose(psTs_k[half][:, k, :],
                                           src_row(t)[:, dc * 128:(dc + 1) * 128],
                                           sidb[:, :])
                    return last
                w = []
                if gidx < 2:
                    w = [("DVE", wait_dve_val), ("ACT", act0)] + list(extra0)
                else:
                    w = [("ACT", evs[gidx - 2])]
                S.add("tensor", tp, waits=w, inc="PE")
                evt = c("PE")
                S.add("scalar",
                      lambda e, dc=dc, g=g, half=half: e.copy(
                          dst_ct[:, dc, g * 512:(g + 1) * 512], psTs[half][:, :]),
                      waits=[("PE", evt)], inc="ACT")
                evs.append(c("ACT"))
                gidx += 1
        return c("PE"), c("ACT")

    prev_proj_mm = [0, 0, 0, 0]

    def emit_projT(L, wsrc, bias_sb, dst, rhs_src, extra_first=()):
        act0 = c("ACT")
        ev_mm = []
        ev_ev = []
        for doc in range(DC):
            slot = doc % 4
            psem = "LD_PW%d" % slot
            wld = [("PE", ev_mm[2 * doc - 7] if doc >= 4 else prev_proj_mm[slot])]
            dma("sync", psem, projw[:, slot, :, :].rearrange("p a b -> p (a b)"),
                wsrc[L, doc].rearrange("p a b -> p (a b)"), waits=wld)
            evw = (psem, c(psem))
            for half in range(2):
                n = 2 * doc + half
                bank = n % 2
                def mm(e, doc=doc, half=half, slot=slot, bank=bank):
                    last = None
                    for dcin in range(DC):
                        last = e.matmul(psS[bank][:, :],
                                        projw[:, slot, dcin, :],
                                        rhs_src[:, dcin, half * 512:(half + 1) * 512],
                                        start=(dcin == 0), stop=(dcin == DC - 1))
                    return last
                w = [evw]
                w.append(("ACT", ev_ev[n - 2] if n >= 2 else act0))
                if n < 2:
                    w.extend(extra_first)
                S.add("tensor", mm, waits=w, inc="PE")
                ev_mm.append(c("PE"))
                S.add("scalar",
                      lambda e, doc=doc, half=half, bank=bank: e.activation(
                          dst[:, doc, half * 512:(half + 1) * 512], psS[bank][:, :],
                          AF.Identity, bias=bias_sb[:, L, doc:doc + 1], scale=1.0),
                      waits=[("PE", ev_mm[-1])], inc="ACT")
                ev_ev.append(c("ACT"))
        prev_proj_mm[0] = ev_mm[-7]
        prev_proj_mm[1] = ev_mm[-5]
        prev_proj_mm[2] = ev_mm[-3]
        prev_proj_mm[3] = ev_mm[-1]
        return ev_mm[-1], ev_ev[-1]

    ev_wprep_pe = [0]
    ev_wprep_dve = [0]
    ev_wprep_act = [0]
    ev_wprep_stg = [0]

    def emit_wprep():
        act0 = c("ACT")
        wchs = [wch, wch2]
        bank_gate = {0: (0, 0), 1: (0, 0)}
        evtp_rt = {-2: 0, -1: 0}
        for rt in range(8):
            csem = "LD_WCH%d" % (rt % 2)
            dma(("sync", "scalar")[rt % 2], csem, wchs[rt % 2][:, :],
                w0s.rearrange("(a p) d -> a p d", p=128)[rt],
                waits=[("PE", evtp_rt[rt - 2])])
            evl = (csem, c(csem))
            for g in range(4):
                bank = g % 2
                def tp(e, rt=rt, g=g, bank=bank):
                    psv = psS[bank].ap().rearrange("p (k d) -> p k d", k=4)
                    last = None
                    for k in range(4):
                        dc = g * 4 + k
                        last = e.transpose(psv[:, k, :],
                                           wchs[rt % 2][:, dc * 128:(dc + 1) * 128],
                                           sidf[:, :])
                    return last
                S.add("tensor", tp, waits=[evl, ("DVE", bank_gate[bank][1]),
                                           ("ACT", bank_gate[bank][0] or act0)], inc="PE")
                evtp_rt[rt] = c("PE")
                evt = c("PE")
                S.add("scalar",
                      lambda e, g=g, bank=bank: e.copy(
                          w0stg[:, 0, g * 4:(g + 1) * 4, :].rearrange("p a b -> p (a b)"),
                          psS[bank][:, :]),
                      waits=[("PE", evt), ("ST_W", c("ST_W"))], inc="ACT")
                evc = c("ACT")
                S.add("vector",
                      lambda e, g=g, bank=bank: e.tensor_sub(
                          w0stg[:, 1, g * 4:(g + 1) * 4, :].rearrange("p a b -> p (a b)"),
                          psS[bank][:, :],
                          w0stg[:, 0, g * 4:(g + 1) * 4, :].rearrange("p a b -> p (a b)")),
                      waits=[("ACT", evc)], inc="DVE")
                bank_gate[bank] = (evc, c("DVE"))
            evd = c("DVE")
            for hl in range(2):
                dma("gpsimd", "ST_W", w_in_v[hl, rt, :, :],
                    w0stg[:, hl, :, :].rearrange("p a b -> p (a b)"),
                    waits=[("DVE", evd)])
        evtp_ht = {-2: evtp_rt[6], -1: evtp_rt[7]}
        for htg in range(4):
            for hh in range(4):
                ht = htg * 4 + hh
                csem = "LD_WCH%d" % (ht % 2)
                dma(("sync", "scalar")[ht % 2], csem, wchs[ht % 2][:, 0:1024],
                    w1s.rearrange("(a p) n -> a p n", p=128)[ht],
                    waits=[("PE", evtp_ht[ht - 2])])
                evl = (csem, c(csem))
                for g in range(2):
                    bank = g % 2
                    def tp(e, ht=ht, g=g, bank=bank):
                        psv = psS[bank].ap().rearrange("p (k d) -> p k d", k=4)
                        last = None
                        for k in range(4):
                            ns = g * 4 + k
                            last = e.transpose(psv[:, k, :],
                                               wchs[ht % 2][:, ns * 128:(ns + 1) * 128],
                                               sidf[:, :])
                        return last
                    S.add("tensor", tp, waits=[evl, ("DVE", bank_gate[bank][1]),
                                               ("ACT", bank_gate[bank][0])], inc="PE")
                    evtp_ht[ht] = c("PE")
                    evt = c("PE")
                    S.add("scalar",
                          lambda e, g=g, hh=hh, bank=bank: e.copy(
                              w1stg[:, 0, g * 4:(g + 1) * 4, hh, :],
                              psS[bank].ap().rearrange("p (k d) -> p k d", k=4)),
                          waits=[("PE", evt), ("ST_W", c("ST_W"))], inc="ACT")
                    evc = c("ACT")
                    S.add("vector",
                          lambda e, g=g, hh=hh, bank=bank: e.tensor_sub(
                              w1stg[:, 1, g * 4:(g + 1) * 4, hh, :],
                              psS[bank].ap().rearrange("p (k d) -> p k d", k=4),
                              w1stg[:, 0, g * 4:(g + 1) * 4, hh, :]),
                          waits=[("ACT", evc)], inc="DVE")
                    bank_gate[bank] = (evc, c("DVE"))
            evd = c("DVE")
            for hl in range(2):
                dma("gpsimd", "ST_W", w_in_pv[2 + hl][:, :, htg * 512:(htg + 1) * 512],
                    w1stg[:, hl, :, :, :].rearrange("p a b c -> p a (b c)"),
                    waits=[("DVE", evd)])
        ev_wprep_pe[0] = c("PE")
        ev_wprep_dve[0] = c("DVE")
        ev_wprep_act[0] = c("ACT")
        ev_wprep_stg[0] = c("ST_W")

    ev_cc = {}
    ev_vlast = {}
    ev_vlast_stg = {}
    ev_stores_kv = {}
    ev_oacc_last = {-1: 0}
    ev_pe_flash_last = {-1: 0}

    ev_qmm_flash = [0]

    def emit_flash(L, ev_qact):
        buf = L % 2
        evS, evSC, evST, evLU, evP, evT, evPTE, evPV, evOA = ({} for _ in range(9))
        evLD, evLDV = {}, {}

        def rs(i):
            return i % 8, i // 8

        def load_k(i):
            r, s = rs(i)
            w = [ev_cc[(L, "k", s)]]
            w.append(("PE", ev_vlast[L] if i == 0 else evS[(i - 1, TJ - 1)]))
            dma("sync", "LD_K", kslot.ap().rearrange("p a b -> p (a b)"),
                g_k_v[s][buf][r], waits=w)
            evLD[i] = c("LD_K")

        def load_v(i):
            r, s = rs(i)
            vs = [vslot, vslot2][i % 2]
            w = [ev_cc[(L, "v", s)]]
            if i == 0:
                w.append(("PE", ev_vlast[L]))
            elif i == 1:
                w.append(("PE", ev_qmm_flash[0]))
            else:
                w.append(("PE", evPV[(i - 2, TJ - 1, 3)]))
            vsem = "LD_V%d" % (i % 2)
            dma("gpsimd", vsem, vs.ap().rearrange("p a b -> p (a b)"),
                g_v_v[s][buf][r], waits=w)
            evLDV[i] = (vsem, c(vsem))

        def emit_pv(i, j):
            idx = i * TJ + j
            vs = [vslot, vslot2][i % 2]
            for db in range(4):
                def pv(e, j=j, db=db, vs=vs):
                    last = None
                    for kc in range(4):
                        last = e.matmul(psPV[:, db * 512:(db + 1) * 512],
                                        ptsb[:, j % 2, kc, :],
                                        vs[:, kc, db * 512:(db + 1) * 512],
                                        start=(kc == 0), stop=(kc == 3))
                    return last
                w = [("ACT", evPTE[idx]), evLDV[i]]
                prev = (i, j - 1, db) if j >= 1 else (i - 1, TJ - 1, db)
                w.append(("DVE", evOA.get(prev, ev_oacc_last[L - 1] if idx == 0 else 0)))
                S.add("tensor", pv, waits=w, inc="PE")
                evPV[(i, j, db)] = c("PE")
                woa = [("PE", evPV[(i, j, db)])]
                if idx == 0 and db == 0:
                    woa.append(("ST_K", ev_stores_kv[L]))
                S.add("vector",
                      lambda e, j=j, db=db: e.scalar_tensor_tensor(
                          oacc[:, j, db * 512:(db + 1) * 512],
                          oacc[:, j, db * 512:(db + 1) * 512],
                          alpha[:, j:j + 1],
                          psPV[:, db * 512:(db + 1) * 512], ALU.mult, ALU.add),
                      waits=woa, inc="DVE")
                evOA[(i, j, db)] = c("DVE")

        pe0_flash = c("PE")
        dve0_flash = c("DVE")
        load_k(0)
        load_v(0)
        for i in range(NSUB):
            pnew, pold = i % 2, 1 - (i % 2)

            def emit_s_sc(i, j, pnew, pold):
                idx = i * TJ + j
                bank = idx % 2
                def smm(e, j=j, bank=bank):
                    last = None
                    for dcin in range(DC):
                        last = e.matmul(psS[bank][:, :],
                                        qT[:, dcin, j * 128:(j + 1) * 128],
                                        kslot[:, dcin, :],
                                        start=(dcin == 0), stop=(dcin == DC - 1))
                    return last
                w = [("LD_K", evLD[i])]
                w.append(("ACT", evSC[idx - 2] if idx >= 2 else ev_qact))
                S.add("tensor", smm, waits=w, inc="PE")
                evS[(i, j)] = c("PE")

                wsc = [("PE", evS[(i, j)])]
                wsc.append(("DVE", evST[idx - 4] if idx >= 4 else dve0_flash))
                S.add("scalar",
                      lambda e, j=j, bank=bank: e.copy(sbf[:, j % 4, :], psS[bank][:, :]),
                      waits=wsc, inc="ACT")
                evSC[idx] = c("ACT")

                S.add("vector",
                      lambda e, j=j: e.tensor_reduce(nmraw[:, j:j + 1], sbf[:, j % 4, :],
                                                     AX.X, ALU.max, negate=True),
                      waits=[("ACT", evSC[idx])])
                S.add("vector", lambda e: e.drain())
                S.add("vector",
                      lambda e, j=j, pnew=pnew, pold=pold: e.tensor_scalar(
                          negm[:, pnew, j:j + 1], nmraw[:, j:j + 1],
                          SCALE, negm[:, pold, j:j + 1], ALU.mult, ALU.min))
                S.add("vector", lambda e: e.drain())
                S.add("vector",
                      lambda e, j=j, pnew=pnew, pold=pold: e.tensor_sub(
                          dcol[:, j:j + 1], negm[:, pnew, j:j + 1],
                          negm[:, pold, j:j + 1]),
                      inc="DVE")
                S.add("vector", lambda e: e.drain())
                evST[idx] = c("DVE")

            def emit_ap(i, j, pnew):
                idx = i * TJ + j
                wal = [("DVE", evST[idx])]
                wal.append(("DVE", evOA.get((i - 1, j, 3), ev_oacc_last[L - 1] if i == 0 else 0)))
                S.add("scalar",
                      lambda e, j=j: e.activation(alpha[:, j:j + 1], dcol[:, j:j + 1],
                                                  AF.Exp, bias=0.0, scale=1.0),
                      waits=wal)
                wp = []
                if idx >= 8:
                    wp.append(("PE", evT[idx - 8]))
                    wp.append(("DVE", evLU[idx - 8]))
                else:
                    wp.append(("PE", pe0_flash))
                    wp.append(("ST_V", ev_vlast_stg[L]))
                S.add("scalar",
                      lambda e, j=j, pnew=pnew: e.activation(
                          pex[:, j, :], sbf[:, j % 4, :], AF.Exp,
                          bias=negm[:, pnew, j:j + 1], scale=SCALE,
                          accum_out=lblk[:, j:j + 1]),
                      waits=wp, inc="ACT")
                evP[idx] = c("ACT")

                S.add("vector",
                      lambda e, j=j: e.scalar_tensor_tensor(
                          lrun[:, j:j + 1], lrun[:, j:j + 1], alpha[:, j:j + 1],
                          lblk[:, j:j + 1], ALU.mult, ALU.add),
                      waits=[("ACT", evP[idx])], inc="DVE")
                evLU[idx] = c("DVE")
                S.add("vector", lambda e: e.drain())

            emit_s_sc(i, 0, pnew, pold)
            for j in range(1, TJ):
                emit_s_sc(i, j, pnew, pold)
                emit_ap(i, j - 1, pnew)
            emit_ap(i, TJ - 1, pnew)
            if i + 1 < NSUB:
                load_k(i + 1)
            def emit_t(j):
                idx = i * TJ + j
                half = j % 2
                def tp(e, j=j, half=half):
                    last = None
                    for k in range(4):
                        last = e.transpose(psTs_k[half][:, k, :],
                                           pex[:, j, k * 128:(k + 1) * 128],
                                           sidb[:, :])
                    return last
                wt = [("ACT", evP[idx])]
                if idx >= 2:
                    wt.append(("ACT", evPTE[idx - 2]))
                S.add("tensor", tp, waits=wt, inc="PE")
                evT[idx] = c("PE")
                wpe = [("PE", evT[idx])]
                prev = (i, j - 2, 3) if j >= 2 else (i - 1, TJ - 1, 3)
                if prev in evPV:
                    wpe.append(("PE", evPV[prev]))
                S.add("scalar",
                      lambda e, j=j, half=half: e.copy(
                          ptsb[:, j % 2, :, :].rearrange("p a b -> p (a b)"),
                          psTs[half][:, :]),
                      waits=wpe, inc="ACT")
                evPTE[idx] = c("ACT")

            emit_t(0)
            emit_t(1)
            for j in range(1, TJ):
                emit_pv(i, j - 1)
                if j + 1 < TJ:
                    emit_t(j + 1)
            emit_pv(i, TJ - 1)
            if i + 1 < NSUB:
                load_v(i + 1)
        ev_oacc_last[L] = evOA[(NSUB - 1, TJ - 1, 3)]
        ev_pe_flash_last[L] = evPV[(NSUB - 1, TJ - 1, 3)]
        return c("GPS")

    def emit_tail(L, ev_gps_flash):
        S.add("vector", lambda e: e.reciprocal(rlv[:, :], lrun[:, :]))
        S.add("vector", lambda e: e.drain(), inc="DVE")
        pe0 = c("PE")
        act0 = c("ACT")
        evN = {}
        evTPlast = {-2: pe0, -1: pe0}
        evE = []
        gi = 0
        for j in range(TJ):
            S.add("vector",
                  lambda e, j=j: e.tensor_scalar(attnj[:, j % 2, :], oacc[:, j, :],
                                                 rlv[:, j:j + 1], None, ALU.mult),
                  waits=[("PE", evTPlast[j - 2])], inc="DVE")
            evN[j] = c("DVE")
            for g in range(4):
                half = gi % 2
                def tp(e, j=j, g=g, half=half):
                    last = None
                    for k in range(4):
                        dc = g * 4 + k
                        last = e.transpose(psTs_k[half][:, k, :],
                                           attnj[:, j % 2, dc * 128:(dc + 1) * 128],
                                           sidb[:, :])
                    return last
                w = [("DVE", evN[j])]
                w.append(("ACT", evE[gi - 2] if gi >= 2 else act0))
                S.add("tensor", tp, waits=w, inc="PE")
                evTPlast[j] = c("PE")
                S.add("scalar",
                      lambda e, j=j, g=g, half=half: e.copy(
                          attnT[:, g * 4:(g + 1) * 4, j * 128:(j + 1) * 128],
                          psTs_k[half][:, :, :]),
                      waits=[("PE", evTPlast[j])], inc="ACT")
                evE.append(c("ACT"))
                gi += 1
        ev_norm_all = evN[TJ - 1]
        for s4 in range(4):
            prev_proj_mm[s4] = max(prev_proj_mm[s4], ev_pe_flash_last[L])
        ev_omm, ev_oact = emit_projT(L, woc, sbo, xT, attnT,
                                     extra_first=[("DVE", ev_norm_all)])
        tail_omm[0] = ev_omm
        if L == NL - 1:
            evEv = []
            gi = 0
            for dc in range(DC):
                for g in range(2):
                    half = gi % 2
                    def tp(e, dc=dc, g=g, half=half):
                        last = None
                        for k in range(4):
                            t = g * 4 + k
                            last = e.transpose(psTs_k[half][:, k, :],
                                               xT[:, dc, t * 128:(t + 1) * 128],
                                               sidb[:, :])
                        return last
                    w = [("ACT", ev_oact)]
                    if gi >= 2:
                        w.append(("DVE", evEv[gi - 2]))
                    S.add("tensor", tp, waits=w, inc="PE")
                    evt = c("PE")
                    S.add("vector",
                          lambda e, dc=dc, g=g, half=half: e.tensor_add(
                              x0rm[:, g * 4:(g + 1) * 4, dc * 128:(dc + 1) * 128],
                              x0rm[:, g * 4:(g + 1) * 4, dc * 128:(dc + 1) * 128],
                              psTs_k[half][:, :, :]),
                          waits=[("PE", evt)], inc="DVE")
                    evEv.append(c("DVE"))
                    gi += 1
            S.add("vector", lambda e: e.drain(), inc="DVE")
        return c("DVE"), ev_norm_all

    ev_resid = 0
    ev_norm_prev = [0]
    tail_omm = [0]
    for L in range(NL):
        if L == 0:
            ev_ln1 = emit_ln(lambda t: x0rm[:, t, :],
                             {0: [("LD_X0", 16)], 4: [("LD_X1", 16)]},
                             {0: [("LD_X0", 16), ("LD_INIT", ev_x0)],
                              4: [("LD_X1", 16)]})
            emit_rm_to_ct(ln_row, xT, ev_ln1,
                          extra0=[("LD_INIT", ev_x0)])

        S.add("gpsimd", lambda e: e.memset(negm[:, 1, :], NEG_BIG),
              waits=[("ACT", c("ACT"))])
        S.add("gpsimd", lambda e: e.drain(), inc="GPS")

        if L == 0:
            emit_wprep()
        for s4 in range(4):
            prev_proj_mm[s4] = max(prev_proj_mm[s4], ev_pe_flash_last[L - 1])
        kx = ([("DVE", ev_norm_prev[0])] if L > 0 else
              [("DVE", ev_wprep_dve[0]), ("ACT", ev_wprep_act[0])])
        ev_kmm, ev_kact = emit_projT(L, wkc, sbk, kTst, xT, extra_first=kx)
        buf = L % 2
        evst_k = []
        for s2 in range(2):
            dma("gpsimd", "ST_K", kv_in_v[s2].rearrange("p (a b) -> p a b", b=512),
                kTst[:, :, s2 * 512:(s2 + 1) * 512],
                waits=[("ACT", ev_kact)])
            evst_k.append(c("ST_K"))

        def issue_kcc(s2):
            S.add("gpsimd",
                  lambda e, s2=s2, buf=buf: e.collective_compute(
                      "AllGather", ALU.bypass, replica_groups=[list(range(NCORE))],
                      ins=[kv_in[s2 * 128:(s2 + 1) * 128, :].opt()],
                      outs=[g_k[s2][buf].opt()]),
                  waits=[("ST_K", evst_k[s2])], inc="CCK%d" % s2)
            ev_cc[(L, "k", s2)] = ("CCK%d" % s2, c("CCK%d" % s2))
        issue_kcc(0)

        dma("sync", "LD_WV", bvb[:, :], bvn[L:L + 1, :],
            waits=[("PE", ev_pe_flash_last[L - 1])])
        wv_v = wvn[L].rearrange("(a p) d -> p a d", p=128)
        w_wv = [("PE", ev_pe_flash_last[L - 1])]
        dma("sync", "LD_WV", wvlo8[:, :, :], wv_v[:, 0:8, :], waits=w_wv)
        w_wvh = ([("PE", ev_wprep_pe[0]), ("ST_W", ev_wprep_stg[0])] if L == 0
                 else [("PE", tail_omm[0])])
        dma("sync", "LD_WV", wvhi8[:, :, :], wv_v[:, 8:16, :], waits=w_wvh)
        ev_wv = c("LD_WV")
        ev_vact = []
        for m in range(TJ):
            def vmm(e, m=m):
                last = None
                for dcin in range(DC):
                    src = wvlo8 if dcin < 8 else wvhi8
                    for db in range(4):
                        last = e.matmul(psPV[:, db * 512:(db + 1) * 512],
                                        xT[:, dcin, m * 128:(m + 1) * 128],
                                        src[:, dcin % 8, db * 512:(db + 1) * 512],
                                        start=(dcin == 0), stop=False)
                for db in range(4):
                    last = e.matmul(psPV[:, db * 512:(db + 1) * 512],
                                    sonb[:, :], bvb[0:1, db * 512:(db + 1) * 512],
                                    start=False, stop=True)
                return last
            w = [("LD_WV", ev_wv)]
            if m == 0:
                w.append(("DVE", ev_oacc_last[L - 1]))
                w.append(("ACT", c("ACT")))
            else:
                w.append(("ACT", ev_vact[-1]))
            S.add("tensor", vmm, waits=w, inc="PE")
            ev_vmm = c("PE")
            base = (m % 2) * 4
            for db in range(4):
                S.add("scalar",
                      lambda e, db=db, base=base: e.copy(
                          pex[:, base + db, :], psPV[:, db * 512:(db + 1) * 512]),
                      waits=([("PE", ev_vmm), ("ST_V", c("ST_V"))]
                             if db == 0 else [("PE", ev_vmm)]), inc="ACT")
            eva = c("ACT")
            dma("gpsimd", "ST_V",
                kv_in_v[2 + m // 4][:, (m % 4) * 2048:(m % 4 + 1) * 2048],
                pex[:, base:base + 4, :], waits=[("ACT", eva)])
            ev_vact.append(c("ACT"))
            if m == 3 or m == 7:
                s2 = m // 4
                evst = c("ST_V")
                S.add("gpsimd",
                      lambda e, s2=s2, buf=buf: e.collective_compute(
                          "AllGather", ALU.bypass, replica_groups=[list(range(NCORE))],
                          ins=[kv_in[(2 + s2) * 128:(3 + s2) * 128, :].opt()],
                          outs=[g_v[s2][buf].opt()]),
                      waits=[("ST_V", evst)], inc="CCV%d" % s2)
                ev_cc[(L, "v", s2)] = ("CCV%d" % s2, c("CCV%d" % s2))
                if m == 3:
                    issue_kcc(1)
        ev_vlast[L] = c("PE")
        ev_vlast_stg[L] = c("ST_V")
        ev_stores_kv[L] = c("ST_K")
        for s3 in ({0: [0], 1: [1], 2: [2], 3: [3]}.get(L, [])):
            S.add("gpsimd",
                  lambda e, s3=s3: e.collective_compute(
                      "AllGather", ALU.bypass, replica_groups=[list(range(NCORE))],
                      ins=[w_in[s3 * 1024:(s3 + 1) * 1024, :].opt()],
                      outs=[w_alls[s3].opt()]),
                  waits=[("ST_W", ev_wprep_stg[0])], inc="CCW")
        ev_cc["Wall"] = ("CCW", 4)

        ev_qmm, ev_qact = emit_projT(L, wqc, sbq, qT, xT)
        ev_qmm_flash[0] = ev_qmm
        S.add("vector", lambda e: e.memset(oacc[:, :, :], 0.0),
              waits=[("PE", ev_qmm), ("ST_K", c("ST_K"))])
        S.add("vector", lambda e: e.drain())
        ev_gps_flash = emit_flash(L, ev_qact)
        ev_resid, ev_na = emit_tail(L, ev_gps_flash)
        ev_norm_prev[0] = ev_na

    ev_ln2 = emit_ln(lambda t: x0rm[:, t, :], [("DVE", ev_resid)],
                     [("PE", c("PE")), ("DVE", ev_resid)])
    emit_rm_to_ct(lambda t: x0rm[:, t, :], xT, ev_resid)
    act0 = c("ACT")
    evX = []
    gi = 0
    for dc in range(DC):
        for g in range(2):
            half = gi % 2
            def tp(e, dc=dc, g=g, half=half):
                last = None
                for k in range(4):
                    t = g * 4 + k
                    last = e.transpose(psTs_k[half][:, k, :],
                                       ln_row(t)[:, dc * 128:(dc + 1) * 128], sidb[:, :])
                return last
            w = [("DVE", ev_ln2), ("ACT", evX[gi - 2] if gi >= 2 else act0)]
            S.add("tensor", tp, waits=w, inc="PE")
            evt = c("PE")
            S.add("scalar",
                  lambda e, dc=dc, g=g, half=half: e.copy(
                      XT[:, dc, g * 512:(g + 1) * 512], psTs[half][:, :]),
                  waits=[("PE", evt)], inc="ACT")
            evX.append(c("ACT"))
            gi += 1
    ev_xtf = c("ACT")

    dve_ffn0 = c("DVE")
    pe_ffn0 = c("PE")
    prev_act_gate = ev_xtf
    evd_hist = []
    evB_all = []
    ev_stg_act_prev = None
    for half in range(2):
        evA = {}
        evGel = {}
        for ncx in range(NC2):
            r, lc = ncx // 8, ncx % 8
            slot = ncx % 2
            fsem = "LD_F0%d" % slot
            wld = [ev_cc["Wall"]]
            wld.append(("PE", evA[ncx - 2] if ncx >= 2 else c("PE")))
            for hl in range(2):
                dma(("sync", "scalar")[hl], fsem, w0buf[:, slot, hl, :],
                    w_all_v[hl][r, lc, :, :],
                    waits=wld if hl == 0 else list(wld))
            evw = (fsem, c(fsem))
            bank = ncx % 2
            def amm(e, slot=slot, bank=bank, half=half):
                last = None
                w0v = w0buf.ap().rearrange("p s l (a b) -> p s l a b", a=DC)
                for dc in range(DC):
                    for hl in range(2):
                        last = e.matmul(psS[bank][:, :],
                                        w0v[:, slot, hl, dc, :],
                                        XT[:, dc, half * 512:(half + 1) * 512],
                                        start=(dc == 0 and hl == 0),
                                        stop=(dc == DC - 1 and hl == 1))
                return last
            w = [evw, ("ACT", prev_act_gate if ncx < 2 else evGel[ncx - 2])]
            S.add("tensor", amm, waits=w, inc="PE")
            evA[ncx] = c("PE")
            dtgt = d16a if ncx < 32 else d16b
            S.add("scalar",
                  lambda e, ncx=ncx, bank=bank, dtgt=dtgt: e.activation(
                      dtgt[:, ncx % 32, :], psS[bank][:, :], AF.Gelu,
                      bias=sb0[:, ncx:ncx + 1], scale=1.0),
                  waits=[("PE", evA[ncx])], inc="ACT")
            evGel[ncx] = c("ACT")
        ev_gelu = c("ACT")
        for pas in range(4):
            for ncx in range(NC2):
                r, lc = ncx // 8, ncx % 8
                g = len(evB_all)
                slot = g % 8
                fsem = "LD_F1%d" % slot
                wld = [("PE", evB_all[g - 8] if g >= 8 else pe_ffn0),
                       ev_cc["Wall"]]
                for hl in range(2):
                    dma(("sync", "scalar")[hl], fsem, w1buf[:, slot, hl, :],
                        w_all_v[2 + hl][r, lc, :, pas * 512:(pas + 1) * 512],
                        waits=wld if hl == 0 else list(wld))
                evw = (fsem, c(fsem))
                dsrc = d16a if ncx < 32 else d16b
                def bmm(e, ncx=ncx, slot=slot, dsrc=dsrc):
                    last = None
                    w1v = w1buf.ap().rearrange("p s l (a b) -> p s l a b", a=4)
                    for hl in range(2):
                        for hc in range(4):
                            last = e.matmul(psPV[:, hc * 512:(hc + 1) * 512],
                                            w1v[:, slot, hl, hc, :],
                                            dsrc[:, ncx % 32, :],
                                            start=(ncx == 0 and hl == 0),
                                            stop=(ncx == NC2 - 1 and hl == 1))
                    return last
                w = [evw, ("ACT", ev_gelu)]
                if ncx == 0:
                    if ev_stg_act_prev is not None:
                        w.append(("ACT", ev_stg_act_prev))
                    else:
                        w.append(("DVE", dve_ffn0))
                        w.append(("ACT", c("ACT")))
                S.add("tensor", bmm, waits=w, inc="PE")
                evB_all.append(c("PE"))
            evB = evB_all[-1]
            stg_l = []
            for hc in range(4):
                hch = pas * 4 + hc
                eslot = hc % 2
                we = [("PE", evB)]
                if len(evd_hist) >= 2:
                    we.append(("DVE", evd_hist[-2]))
                S.add("scalar",
                      lambda e, hc=hc, hch=hch, eslot=eslot: e.activation(
                          e16s[:, eslot, :], psPV[:, hc * 512:(hc + 1) * 512],
                          AF.Identity, bias=sb1[:, hch:hch + 1], scale=1.0),
                      waits=we, inc="ACT")
                eva = c("ACT")
                wadd = [("ACT", eva), ("ST_OUT", c("ST_OUT"))]
                S.add("vector",
                      lambda e, hch=hch, half=half, eslot=eslot, oslot=hc % 2: e.tensor_add(
                          outb[:, oslot, :],
                          xT[:, hch, half * 512:(half + 1) * 512],
                          e16s[:, eslot, :]),
                      waits=wadd, inc="DVE")
                evd = c("DVE")
                evd_hist.append(evd)
                dma("gpsimd", "ST_OUT",
                    y_t.rearrange("(a p) r -> a p r", p=128)[pas * 4 + hc]
                       [:, half * 512:(half + 1) * 512],
                    outb[:, hc % 2, :], waits=[("DVE", evd)])
                stg_l.append(c("ST_OUT"))
            ev_stg_act_prev = c("ACT")
        prev_act_gate = c("ACT")
    ev_out = c("ST_OUT")
    S.add("gpsimd", None, waits=[("ST_OUT", ev_out)])

    from contextlib import ExitStack
    sems = {}
    es = ExitStack()
    with es:
        for sname in ("LD_INIT", "LD_X0", "LD_X1",
                      "LD_PW0", "LD_PW1", "LD_PW2", "LD_PW3",
                      "LD_WV", "LD_WCH0", "LD_WCH1",
                      "LD_K", "LD_V0", "LD_V1", "LD_F00", "LD_F01",
                      "LD_F10", "LD_F11", "LD_F12", "LD_F13",
                      "LD_F14", "LD_F15", "LD_F16", "LD_F17",
                      "ST_K", "ST_V", "ST_W", "ST_OUT",
                      "CCK0", "CCK1", "CCV0", "CCV1", "CCW",
                      "PE", "ACT", "DVE", "GPS"):
            sems[sname] = es.enter_context(nc.semaphore(sname))
        blk = es.enter_context(nc.Block())
        S.emit(blk, sems)
    n_items = {e: len(S.items[e]) for e in ENGS}
    return nc, n_items


_NC_CACHE = None


def kernel(**inputs):
    global _NC_CACHE
    xs = np.asarray(inputs["L0_attn_x"])
    Wq = np.asarray(inputs["L0_attn_Wq"])
    bq = np.asarray(inputs["L0_attn_bq"])
    Wk = np.asarray(inputs["L0_attn_Wk"])
    bk = np.asarray(inputs["L0_attn_bk"])
    Wv = np.asarray(inputs["L0_attn_Wv"])
    bv = np.asarray(inputs["L0_attn_bv"])
    Wo = np.asarray(inputs["L0_attn_Wo"])
    bo = np.asarray(inputs["L0_attn_bo"])
    W0 = np.asarray(inputs["L0_ffn_W0"], dtype=np.float32)
    bias0 = np.asarray(inputs["L0_ffn_bias0"], dtype=np.float32)
    W1 = np.asarray(inputs["L0_ffn_W1"], dtype=np.float32)
    bias1 = np.asarray(inputs["L0_ffn_bias1"], dtype=np.float32)

    x2 = np.ascontiguousarray(xs.reshape(SEQ, D))

    def colchunk(W):
        return np.ascontiguousarray(
            W.reshape(NL, DC, 128, DC, 128).transpose(0, 3, 2, 1, 4))

    def biasrc(b):
        return np.ascontiguousarray(b.reshape(NL, DC, 128).transpose(2, 0, 1))

    common = {
        "wqc": colchunk(Wq), "wkc": colchunk(Wk), "woc": colchunk(Wo),
        "wvn": np.ascontiguousarray(Wv),
        "bqr": biasrc(bq), "bkr": biasrc(bk), "bor": biasrc(bo),
        "bvn": np.ascontiguousarray(bv),
        "b0r": np.ascontiguousarray(bias0.reshape(NC2, 128).T),
        "b1r": np.ascontiguousarray(bias1.reshape(DC, 128).T),
        "identb": np.eye(128, dtype=ml_dtypes.bfloat16),
        "identf": np.eye(128, dtype=np.float32),
        "onesb": np.ones((1, 128), dtype=ml_dtypes.bfloat16),
    }
    in_maps = []
    for cid in range(NCORE):
        m = dict(common)
        m["x"] = np.ascontiguousarray(x2[cid * RPC:(cid + 1) * RPC])
        m["w0s"] = np.ascontiguousarray(W0[cid * RPC:(cid + 1) * RPC])
        m["w1s"] = np.ascontiguousarray(W1[:, cid * RPC:(cid + 1) * RPC])
        in_maps.append(m)

    if _NC_CACHE is None:
        _NC_CACHE = build_nc()[0]
    res = run_bass_kernel_spmd(_NC_CACHE, in_maps, core_ids=list(range(NCORE)))
    parts = [np.asarray(res.results[cid]["y_t"]).T for cid in range(NCORE)]
    out = np.concatenate(parts, axis=0).reshape(1, SEQ, D)
    return out


if __name__ == "__main__":
    nc, n = build_nc()
    print("built ok; items:", n, "total:", sum(n.values()))


# version 19
# speedup vs baseline: 1.0435x; 1.0435x over previous
import math
import numpy as np
import ml_dtypes
import concourse.bass as bass
import concourse.mybir as mybir
from concourse.bass_utils import run_bass_kernel_spmd

F32 = mybir.dt.float32
BF16 = mybir.dt.bfloat16
F16 = mybir.dt.float16
AX = mybir.AxisListType
ALU = mybir.AluOpType
AF = mybir.ActivationFunctionType

NCORE = 8
SEQ = 8192
D = 2048
RPC = SEQ // NCORE
NL = 4
NFF = 8192
DC = D // 128
TJ = RPC // 128
NSUB = 16
NC2 = NFF // 128
SCALE = 1.0 / math.sqrt(D)
NEG_BIG = 3.0e38

ENGS = ("sync", "tensor", "vector", "scalar", "gpsimd")


class Sched:

    def __init__(self):
        self.items = {e: [] for e in ENGS}
        self.cnt = {}

    def c(self, sem):
        return self.cnt.get(sem, 0)

    def add(self, eng, fn, waits=(), inc=None, inck=1):
        val = None
        if inc is not None:
            self.cnt[inc] = self.cnt.get(inc, 0) + inck
            val = self.cnt[inc]
        ws = [(s, v) for (s, v) in waits if v and v > 0]
        self.items[eng].append((fn, ws, inc, inck))
        return val

    def emit(self, blk, sems):
        def mk(name):
            def run(eng):
                seen = {}
                for fn, waits, inc, inck in self.items[name]:
                    for (s, v) in waits:
                        if seen.get(s, 0) < v:
                            eng.wait_ge(sems[s], v)
                            seen[s] = v
                    inst = fn(eng) if fn is not None else None
                    if inc is not None:
                        if inst is not None:
                            inst.then_inc(sems[inc], inck)
                        else:
                            eng.sem_inc(sems[inc], inck)
            return run
        blk.sync(mk("sync"))
        blk.tensor(mk("tensor"))
        blk.vector(mk("vector"))
        blk.scalar(mk("scalar"))
        blk.gpsimd(mk("gpsimd"))


def build_nc():
    nc = bass.Bass("TRN2", target_bir_lowering=False)

    def ein(name, shape, dt):
        return nc.dram_tensor(name, shape, dt, kind="ExternalInput").ap()

    x_in = ein("x", [RPC, D], BF16)
    wqc = ein("wqc", [NL, DC, 128, DC, 128], BF16)
    wkc = ein("wkc", [NL, DC, 128, DC, 128], BF16)
    woc = ein("woc", [NL, DC, 128, DC, 128], BF16)
    wvn = ein("wvn", [NL, D, D], BF16)
    bqr = ein("bqr", [128, NL, DC], BF16)
    bkr = ein("bkr", [128, NL, DC], BF16)
    bor = ein("bor", [128, NL, DC], BF16)
    bvn = ein("bvn", [NL, D], BF16)
    w0s = ein("w0s", [RPC, D], F32)
    w1s = ein("w1s", [D, RPC], F32)
    b0r = ein("b0r", [128, NC2], F32)
    b1r = ein("b1r", [128, DC], F32)
    identb = ein("identb", [128, 128], BF16)
    identf = ein("identf", [128, 128], F32)
    onesb = ein("onesb", [1, 128], BF16)

    y_t = nc.dram_tensor("y_t", [D, RPC], F32, kind="ExternalOutput").ap()

    kv_in = nc.dram_tensor("kv_in", [4 * 128, SEQ], BF16).ap()
    g_k = [[nc.dram_tensor(f"g_k{s}{b}", [NCORE * 128, SEQ], BF16,
                           addr_space="Shared").ap()
            for b in range(2)] for s in range(2)]
    g_v = [[nc.dram_tensor(f"g_v{s}{b}", [NCORE * 128, SEQ], BF16,
                           addr_space="Shared").ap()
            for b in range(2)] for s in range(2)]
    w_in = nc.dram_tensor("w_in", [4 * 8 * 128, D], F16).ap()
    w_alls = [nc.dram_tensor(f"w_all{s}", [NCORE * 8 * 128, D], F16,
                             addr_space="Shared").ap()
              for s in range(4)]

    kv_in_v = kv_in.rearrange("(a p) n -> a p n", p=128)
    g_k_v = [[g_k[s][b].rearrange("(r p) n -> r p n", p=128) for b in range(2)] for s in range(2)]
    g_v_v = [[g_v[s][b].rearrange("(r p) n -> r p n", p=128) for b in range(2)] for s in range(2)]
    w_in_v = w_in.rearrange("(s l p) h -> s l p h", l=8, p=128)
    w_in_pv = w_in.rearrange("(s l p) h -> s p l h", l=8, p=128)
    w_all_v = [w_alls[s].rearrange("(r l p) h -> r l p h", l=8, p=128)
               for s in range(4)]

    def sb(name, shape, dt, off):
        return nc.alloc_sbuf_tensor_at(name, shape, dt, offset=off)

    BASE = 16512
    x0rm = sb("x0rm", [128, TJ, D], BF16, BASE)
    XT = sb("XT", [128, DC, RPC], F16, BASE)
    RA = BASE + 32768
    lnout = sb("lnout", [128, TJ, D], BF16, RA)
    kTst = sb("kTst", [128, DC, RPC], BF16, RA)
    d16b = sb("d16b", [128, 32, 512], F16, RA)
    oacc = sb("oacc", [128, TJ, D], F32, RA)
    xT = sb("xT", [128, DC, RPC], BF16, RA + 32768)
    RB = RA + 65536
    qT = sb("qT", [128, DC, RPC], BF16, RB)
    attnT = sb("attnT", [128, DC, RPC], BF16, RB)
    wch = sb("wch", [128, D], F32, RB)
    w0stg = sb("w0stg", [128, 2, DC, 128], F16, RB + 8192)
    w1stg = sb("w1stg", [128, 2, 8, 4, 128], F16, RB + 16384)
    wvhi8 = sb("wvhi8", [128, 8, D], BF16, RB)
    w0buf = sb("w0buf", [128, 2, 2, DC * 128], F16, RB)
    w1buf = sb("w1buf", [128, 8, 2, 512], F16, RB + 16384)
    KV = RB + 32768
    kslot = sb("kslot", [128, DC, 512], BF16, KV)
    vslot = sb("vslot", [128, 4, D], BF16, KV + 16384)
    wvlo8 = sb("wvlo8", [128, 8, D], BF16, KV)
    d16a = sb("d16a", [128, 32, 512], F16, KV)
    SM = KV + 32768
    projw = sb("projw", [128, 4, DC, 128], BF16, SM)
    pex = sb("pex", [128, TJ, 512], BF16, SM + 16384)
    e16s = sb("e16s", [128, 4, 512], F16, SM + 16384)
    ptsb = sb("ptsb", [128, 2, 4, 128], BF16, SM + 24576)
    attnj = sb("attnj", [128, 2, D], BF16, SM + 26624)
    vslot2 = sb("vslot2", [128, 4, D], BF16, SM)
    wch2 = sb("wch2", [128, D], F32, SM + 26624)
    outb = sb("outb", [128, 4, 512], F32, SM + 26624)
    bvb = sb("bvb", [1, D], BF16, SM + 34816)
    sbq = sb("sbq", [128, NL, DC], BF16, SM + 38912)
    sbk = sb("sbk", [128, NL, DC], BF16, SM + 39040)
    sbo = sb("sbo", [128, NL, DC], BF16, SM + 39168)
    sb0 = sb("sb0", [128, NC2], F32, SM + 39296)
    sb1 = sb("sb1", [128, DC], F32, SM + 39552)
    sidb = sb("sidb", [128, 128], BF16, SM + 39616)
    sidf = sb("sidf", [128, 128], F32, SM + 39872)
    sonb = sb("sonb", [1, 128], BF16, SM + 40384)
    ST = SM + 40640
    negm = sb("negm", [128, 2, TJ], F32, ST)
    nmraw = sb("nmraw", [128, TJ], F32, ST + 64)
    dcol = sb("dcol", [128, TJ], F32, ST + 96)
    alpha = sb("alpha", [128, TJ], F32, ST + 128)
    lblk = sb("lblk", [128, TJ], F32, ST + 160)
    lrun = sb("lrun", [128, TJ], F32, ST + 192)
    rlv = sb("rlv", [128, TJ], F32, ST + 224)
    lnsum = sb("lnsum", [128, TJ], F32, ST + 256)
    lnsq = sb("lnsq", [128, TJ], F32, ST + 288)
    lnm = sb("lnm", [128, TJ], F32, ST + 320)
    lnr = sb("lnr", [128, TJ], F32, ST + 352)
    sbf = sb("sbf", [128, 4, 512], BF16, ST + 512)
    assert ST + 512 + 4096 <= 229376

    psS = [nc.alloc_psum_tensor("psS0", [128, 512], F32),
           nc.alloc_psum_tensor("psS1", [128, 512], F32)]
    psTs = [nc.alloc_psum_tensor("psT0", [128, 512], BF16),
            nc.alloc_psum_tensor("psT1", [128, 512], BF16)]
    psPV = nc.alloc_psum_tensor("psPV", [128, 2048], F32)
    psTs_k = [t.ap().rearrange("p (k d) -> p k d", k=4) for t in psTs]

    S = Sched()
    c = S.c

    def dma(eng, sem, out, in_, waits=()):
        return S.add(eng, lambda e, o=out, i=in_: e.dma_start(out=o, in_=i),
                     waits=waits, inc=sem, inck=16)

    x_in_v = x_in.rearrange("(t p) d -> p t d", p=128)
    for h in range(2):
        dma("sync", "LD_X%d" % h, x0rm[:, 4 * h:4 * h + 4, :],
            x_in_v[:, 4 * h:4 * h + 4, :])
    dma("sync", "LD_INIT", sbq[:, :, :], bqr)
    dma("sync", "LD_INIT", sbk[:, :, :], bkr)
    dma("sync", "LD_INIT", sbo[:, :, :], bor)
    dma("sync", "LD_INIT", sb0[:, :], b0r)
    dma("sync", "LD_INIT", sb1[:, :], b1r)
    dma("sync", "LD_INIT", sidb[:, :], identb)
    dma("sync", "LD_INIT", sidf[:, :], identf)
    dma("sync", "LD_INIT", sonb[:, :], onesb)
    ev_x0 = c("LD_INIT")

    S.add("gpsimd", lambda e: e.memset(negm[:, :, :], NEG_BIG))
    S.add("gpsimd", lambda e: e.memset(lrun[:, :], 0.0))
    S.add("gpsimd", lambda e: e.drain(), inc="GPS")

    def ln_row(t):
        return lnout[:, t, :]

    def emit_ln(src_row, wait_dve, wait_act):
        wd = wait_dve if isinstance(wait_dve, dict) else {0: wait_dve}
        wa = wait_act if isinstance(wait_act, dict) else {0: wait_act}
        for t in range(TJ):
            S.add("vector",
                  lambda e, t=t: e.reduce_sum(lnsum[:, t:t + 1], src_row(t), AX.X),
                  waits=wd.get(t, []), inc="DVE")
        ev_sums = c("DVE")
        for t in range(TJ):
            S.add("scalar",
                  lambda e, t=t: e.activation(ln_row(t), src_row(t), AF.Square,
                                              bias=0.0, scale=1.0,
                                              accum_out=lnsq[:, t:t + 1]),
                  waits=wa.get(t, []), inc="ACT")
        ev_sq = c("ACT")
        S.add("vector", lambda e: e.tensor_scalar(lnm[:, :], lnsum[:, :], 1.0 / D, None, ALU.mult),
              waits=[("DVE", ev_sums)])
        S.add("vector", lambda e: e.drain())
        S.add("vector", lambda e: e.tensor_scalar(lnr[:, :], lnsq[:, :], 1.0 / D, None, ALU.mult),
              waits=[("ACT", ev_sq)])
        S.add("vector", lambda e: e.drain())
        S.add("vector", lambda e: e.tensor_mul(lnsq[:, :], lnm[:, :], lnm[:, :]))
        S.add("vector", lambda e: e.drain())
        S.add("vector", lambda e: e.tensor_sub(lnr[:, :], lnr[:, :], lnsq[:, :]))
        S.add("vector", lambda e: e.drain())
        S.add("vector", lambda e: e.tensor_scalar(lnr[:, :], lnr[:, :], 1e-5, None, ALU.add))
        S.add("vector", lambda e: e.drain(), inc="DVE")
        ev_var = c("DVE")
        S.add("scalar", lambda e: e.activation(lnr[:, :], lnr[:, :], AF.Ln, bias=0.0, scale=1.0),
              waits=[("DVE", ev_var)])
        S.add("scalar", lambda e: e.drain())
        S.add("scalar", lambda e: e.activation(lnr[:, :], lnr[:, :], AF.Exp, bias=0.0, scale=-0.5))
        S.add("scalar", lambda e: e.drain(), inc="ACT")
        ev_rstd = c("ACT")
        for t in range(TJ):
            S.add("vector",
                  lambda e, t=t: e.tensor_scalar(ln_row(t), src_row(t),
                                                 lnm[:, t:t + 1], lnr[:, t:t + 1],
                                                 ALU.subtract, ALU.mult),
                  waits=[("ACT", ev_rstd)] if t == 0 else [], inc="DVE")
        return c("DVE")

    def emit_rm_to_ct(src_row, dst_ct, wait_dve_val, extra0=()):
        act0 = c("ACT")
        evs = []
        gidx = 0
        for dc in range(DC):
            for g in range(2):
                half = gidx % 2
                def tp(e, dc=dc, g=g, half=half):
                    last = None
                    for k in range(4):
                        t = g * 4 + k
                        last = e.transpose(psTs_k[half][:, k, :],
                                           src_row(t)[:, dc * 128:(dc + 1) * 128],
                                           sidb[:, :])
                    return last
                w = []
                if gidx < 2:
                    w = [("DVE", wait_dve_val), ("ACT", act0)] + list(extra0)
                else:
                    w = [("ACT", evs[gidx - 2])]
                S.add("tensor", tp, waits=w, inc="PE")
                evt = c("PE")
                S.add("scalar",
                      lambda e, dc=dc, g=g, half=half: e.copy(
                          dst_ct[:, dc, g * 512:(g + 1) * 512], psTs[half][:, :]),
                      waits=[("PE", evt)], inc="ACT")
                evs.append(c("ACT"))
                gidx += 1
        return c("PE"), c("ACT")

    prev_proj_mm = [0, 0, 0, 0]

    def emit_projT(L, wsrc, bias_sb, dst, rhs_src, extra_first=()):
        act0 = c("ACT")
        ev_mm = []
        ev_ev = []
        for doc in range(DC):
            slot = doc % 4
            psem = "LD_PW%d" % slot
            wld = [("PE", ev_mm[2 * doc - 7] if doc >= 4 else prev_proj_mm[slot])]
            dma("sync", psem, projw[:, slot, :, :].rearrange("p a b -> p (a b)"),
                wsrc[L, doc].rearrange("p a b -> p (a b)"), waits=wld)
            evw = (psem, c(psem))
            for half in range(2):
                n = 2 * doc + half
                bank = n % 2
                def mm(e, doc=doc, half=half, slot=slot, bank=bank):
                    last = None
                    for dcin in range(DC):
                        last = e.matmul(psS[bank][:, :],
                                        projw[:, slot, dcin, :],
                                        rhs_src[:, dcin, half * 512:(half + 1) * 512],
                                        start=(dcin == 0), stop=(dcin == DC - 1))
                    return last
                w = [evw]
                w.append(("ACT", ev_ev[n - 2] if n >= 2 else act0))
                if n < 2:
                    w.extend(extra_first)
                S.add("tensor", mm, waits=w, inc="PE")
                ev_mm.append(c("PE"))
                S.add("scalar",
                      lambda e, doc=doc, half=half, bank=bank: e.activation(
                          dst[:, doc, half * 512:(half + 1) * 512], psS[bank][:, :],
                          AF.Identity, bias=bias_sb[:, L, doc:doc + 1], scale=1.0),
                      waits=[("PE", ev_mm[-1])], inc="ACT")
                ev_ev.append(c("ACT"))
        prev_proj_mm[0] = ev_mm[-7]
        prev_proj_mm[1] = ev_mm[-5]
        prev_proj_mm[2] = ev_mm[-3]
        prev_proj_mm[3] = ev_mm[-1]
        return ev_mm[-1], ev_ev[-1]

    ev_wprep_pe = [0]
    ev_wprep_dve = [0]
    ev_wprep_act = [0]
    ev_wprep_stg = [0]

    def emit_wprep():
        act0 = c("ACT")
        wchs = [wch, wch2]
        bank_gate = {0: (0, 0), 1: (0, 0)}
        evtp_rt = {-2: 0, -1: 0}
        for rt in range(8):
            csem = "LD_WCH%d" % (rt % 2)
            dma("sync", csem, wchs[rt % 2][:, :],
                w0s.rearrange("(a p) d -> a p d", p=128)[rt],
                waits=[("PE", evtp_rt[rt - 2])])
            evl = (csem, c(csem))
            for g in range(4):
                bank = g % 2
                def tp(e, rt=rt, g=g, bank=bank):
                    psv = psS[bank].ap().rearrange("p (k d) -> p k d", k=4)
                    last = None
                    for k in range(4):
                        dc = g * 4 + k
                        last = e.transpose(psv[:, k, :],
                                           wchs[rt % 2][:, dc * 128:(dc + 1) * 128],
                                           sidf[:, :])
                    return last
                S.add("tensor", tp, waits=[evl, ("DVE", bank_gate[bank][1]),
                                           ("ACT", bank_gate[bank][0] or act0)], inc="PE")
                evtp_rt[rt] = c("PE")
                evt = c("PE")
                S.add("scalar",
                      lambda e, g=g, bank=bank: e.copy(
                          w0stg[:, 0, g * 4:(g + 1) * 4, :].rearrange("p a b -> p (a b)"),
                          psS[bank][:, :]),
                      waits=[("PE", evt), ("ST_W", c("ST_W"))], inc="ACT")
                evc = c("ACT")
                S.add("vector",
                      lambda e, g=g, bank=bank: e.tensor_sub(
                          w0stg[:, 1, g * 4:(g + 1) * 4, :].rearrange("p a b -> p (a b)"),
                          psS[bank][:, :],
                          w0stg[:, 0, g * 4:(g + 1) * 4, :].rearrange("p a b -> p (a b)")),
                      waits=[("ACT", evc)], inc="DVE")
                bank_gate[bank] = (evc, c("DVE"))
            evd = c("DVE")
            for hl in range(2):
                dma("gpsimd", "ST_W", w_in_v[hl, rt, :, :],
                    w0stg[:, hl, :, :].rearrange("p a b -> p (a b)"),
                    waits=[("DVE", evd)])
        evtp_ht = {-2: evtp_rt[6], -1: evtp_rt[7]}
        for htg in range(4):
            for hh in range(4):
                ht = htg * 4 + hh
                csem = "LD_WCH%d" % (ht % 2)
                dma("sync", csem, wchs[ht % 2][:, 0:1024],
                    w1s.rearrange("(a p) n -> a p n", p=128)[ht],
                    waits=[("PE", evtp_ht[ht - 2])])
                evl = (csem, c(csem))
                for g in range(2):
                    bank = g % 2
                    def tp(e, ht=ht, g=g, bank=bank):
                        psv = psS[bank].ap().rearrange("p (k d) -> p k d", k=4)
                        last = None
                        for k in range(4):
                            ns = g * 4 + k
                            last = e.transpose(psv[:, k, :],
                                               wchs[ht % 2][:, ns * 128:(ns + 1) * 128],
                                               sidf[:, :])
                        return last
                    S.add("tensor", tp, waits=[evl, ("DVE", bank_gate[bank][1]),
                                               ("ACT", bank_gate[bank][0])], inc="PE")
                    evtp_ht[ht] = c("PE")
                    evt = c("PE")
                    S.add("scalar",
                          lambda e, g=g, hh=hh, bank=bank: e.copy(
                              w1stg[:, 0, g * 4:(g + 1) * 4, hh, :],
                              psS[bank].ap().rearrange("p (k d) -> p k d", k=4)),
                          waits=[("PE", evt), ("ST_W", c("ST_W"))], inc="ACT")
                    evc = c("ACT")
                    S.add("vector",
                          lambda e, g=g, hh=hh, bank=bank: e.tensor_sub(
                              w1stg[:, 1, g * 4:(g + 1) * 4, hh, :],
                              psS[bank].ap().rearrange("p (k d) -> p k d", k=4),
                              w1stg[:, 0, g * 4:(g + 1) * 4, hh, :]),
                          waits=[("ACT", evc)], inc="DVE")
                    bank_gate[bank] = (evc, c("DVE"))
            evd = c("DVE")
            for hl in range(2):
                dma("gpsimd", "ST_W", w_in_pv[2 + hl][:, :, htg * 512:(htg + 1) * 512],
                    w1stg[:, hl, :, :, :].rearrange("p a b c -> p a (b c)"),
                    waits=[("DVE", evd)])
        ev_wprep_pe[0] = c("PE")
        ev_wprep_dve[0] = c("DVE")
        ev_wprep_act[0] = c("ACT")
        ev_wprep_stg[0] = c("ST_W")

    ev_cc = {}
    ev_vlast = {}
    ev_vlast_stg = {}
    ev_stores_kv = {}
    ev_oacc_last = {-1: 0}
    ev_pe_flash_last = {-1: 0}

    ev_qmm_flash = [0]

    def emit_flash(L, ev_qact):
        buf = L % 2
        evS, evSC, evST, evLU, evP, evT, evPTE, evPV, evOA = ({} for _ in range(9))
        evLD, evLDV = {}, {}

        def rs(i):
            return i % 8, i // 8

        def load_k(i):
            r, s = rs(i)
            w = [ev_cc[(L, "k", s)]]
            w.append(("PE", ev_vlast[L] if i == 0 else evS[(i - 1, TJ - 1)]))
            dma("sync", "LD_K", kslot.ap().rearrange("p a b -> p (a b)"),
                g_k_v[s][buf][r], waits=w)
            evLD[i] = c("LD_K")

        def load_v(i):
            r, s = rs(i)
            vs = [vslot, vslot2][i % 2]
            w = [ev_cc[(L, "v", s)]]
            if i == 0:
                w.append(("PE", ev_vlast[L]))
            elif i == 1:
                w.append(("PE", ev_qmm_flash[0]))
            else:
                w.append(("PE", evPV[(i - 2, TJ - 1, 3)]))
            vsem = "LD_V%d" % (i % 2)
            dma("sync" if i < 2 else "scalar", vsem,
                vs.ap().rearrange("p a b -> p (a b)"),
                g_v_v[s][buf][r], waits=w)
            evLDV[i] = (vsem, c(vsem))

        def emit_pv(i, j):
            idx = i * TJ + j
            vs = [vslot, vslot2][i % 2]
            for db in range(4):
                def pv(e, j=j, db=db, vs=vs):
                    last = None
                    for kc in range(4):
                        last = e.matmul(psPV[:, db * 512:(db + 1) * 512],
                                        ptsb[:, j % 2, kc, :],
                                        vs[:, kc, db * 512:(db + 1) * 512],
                                        start=(kc == 0), stop=(kc == 3))
                    return last
                w = [("ACT", evPTE[idx]), evLDV[i]]
                prev = (i, j - 1, db) if j >= 1 else (i - 1, TJ - 1, db)
                w.append(("DVE", evOA.get(prev, ev_oacc_last[L - 1] if idx == 0 else 0)))
                S.add("tensor", pv, waits=w, inc="PE")
                evPV[(i, j, db)] = c("PE")
                woa = [("PE", evPV[(i, j, db)])]
                if idx == 0 and db == 0:
                    woa.append(("ST_K", ev_stores_kv[L]))
                S.add("vector",
                      lambda e, j=j, db=db: e.scalar_tensor_tensor(
                          oacc[:, j, db * 512:(db + 1) * 512],
                          oacc[:, j, db * 512:(db + 1) * 512],
                          alpha[:, j:j + 1],
                          psPV[:, db * 512:(db + 1) * 512], ALU.mult, ALU.add),
                      waits=woa, inc="DVE")
                evOA[(i, j, db)] = c("DVE")

        pe0_flash = c("PE")
        dve0_flash = c("DVE")
        load_k(0)
        load_v(0)
        for i in range(NSUB):
            pnew, pold = i % 2, 1 - (i % 2)

            def emit_s_sc(i, j, pnew, pold):
                idx = i * TJ + j
                bank = idx % 2
                def smm(e, j=j, bank=bank):
                    last = None
                    for dcin in range(DC):
                        last = e.matmul(psS[bank][:, :],
                                        qT[:, dcin, j * 128:(j + 1) * 128],
                                        kslot[:, dcin, :],
                                        start=(dcin == 0), stop=(dcin == DC - 1))
                    return last
                w = [("LD_K", evLD[i])]
                w.append(("ACT", evSC[idx - 2] if idx >= 2 else ev_qact))
                S.add("tensor", smm, waits=w, inc="PE")
                evS[(i, j)] = c("PE")

                wsc = [("PE", evS[(i, j)])]
                wsc.append(("DVE", evST[idx - 4] if idx >= 4 else dve0_flash))
                S.add("scalar",
                      lambda e, j=j, bank=bank: e.copy(sbf[:, j % 4, :], psS[bank][:, :]),
                      waits=wsc, inc="ACT")
                evSC[idx] = c("ACT")

                S.add("vector",
                      lambda e, j=j: e.tensor_reduce(nmraw[:, j:j + 1], sbf[:, j % 4, :],
                                                     AX.X, ALU.max, negate=True),
                      waits=[("ACT", evSC[idx])])
                S.add("vector", lambda e: e.drain())
                S.add("vector",
                      lambda e, j=j, pnew=pnew, pold=pold: e.tensor_scalar(
                          negm[:, pnew, j:j + 1], nmraw[:, j:j + 1],
                          SCALE, negm[:, pold, j:j + 1], ALU.mult, ALU.min))
                S.add("vector", lambda e: e.drain())
                S.add("vector",
                      lambda e, j=j, pnew=pnew, pold=pold: e.tensor_sub(
                          dcol[:, j:j + 1], negm[:, pnew, j:j + 1],
                          negm[:, pold, j:j + 1]),
                      inc="DVE")
                S.add("vector", lambda e: e.drain())
                evST[idx] = c("DVE")

            def emit_ap(i, j, pnew):
                idx = i * TJ + j
                wal = [("DVE", evST[idx])]
                wal.append(("DVE", evOA.get((i - 1, j, 3), ev_oacc_last[L - 1] if i == 0 else 0)))
                S.add("scalar",
                      lambda e, j=j: e.activation(alpha[:, j:j + 1], dcol[:, j:j + 1],
                                                  AF.Exp, bias=0.0, scale=1.0),
                      waits=wal)
                wp = []
                if idx >= 8:
                    wp.append(("PE", evT[idx - 8]))
                    wp.append(("DVE", evLU[idx - 8]))
                else:
                    wp.append(("PE", pe0_flash))
                    wp.append(("ST_V", ev_vlast_stg[L]))
                S.add("scalar",
                      lambda e, j=j, pnew=pnew: e.activation(
                          pex[:, j, :], sbf[:, j % 4, :], AF.Exp,
                          bias=negm[:, pnew, j:j + 1], scale=SCALE,
                          accum_out=lblk[:, j:j + 1]),
                      waits=wp, inc="ACT")
                evP[idx] = c("ACT")

                S.add("vector",
                      lambda e, j=j: e.scalar_tensor_tensor(
                          lrun[:, j:j + 1], lrun[:, j:j + 1], alpha[:, j:j + 1],
                          lblk[:, j:j + 1], ALU.mult, ALU.add),
                      waits=[("ACT", evP[idx])], inc="DVE")
                evLU[idx] = c("DVE")
                S.add("vector", lambda e: e.drain())

            emit_s_sc(i, 0, pnew, pold)
            for j in range(1, TJ):
                emit_s_sc(i, j, pnew, pold)
                emit_ap(i, j - 1, pnew)
            emit_ap(i, TJ - 1, pnew)
            if i + 1 < NSUB:
                load_k(i + 1)
            def emit_t(j):
                idx = i * TJ + j
                half = j % 2
                def tp(e, j=j, half=half):
                    last = None
                    for k in range(4):
                        last = e.transpose(psTs_k[half][:, k, :],
                                           pex[:, j, k * 128:(k + 1) * 128],
                                           sidb[:, :])
                    return last
                wt = [("ACT", evP[idx])]
                if idx >= 2:
                    wt.append(("ACT", evPTE[idx - 2]))
                S.add("tensor", tp, waits=wt, inc="PE")
                evT[idx] = c("PE")
                wpe = [("PE", evT[idx])]
                prev = (i, j - 2, 3) if j >= 2 else (i - 1, TJ - 1, 3)
                if prev in evPV:
                    wpe.append(("PE", evPV[prev]))
                S.add("scalar",
                      lambda e, j=j, half=half: e.copy(
                          ptsb[:, j % 2, :, :].rearrange("p a b -> p (a b)"),
                          psTs[half][:, :]),
                      waits=wpe, inc="ACT")
                evPTE[idx] = c("ACT")

            emit_t(0)
            emit_t(1)
            for j in range(1, TJ):
                emit_pv(i, j - 1)
                if j + 1 < TJ:
                    emit_t(j + 1)
            emit_pv(i, TJ - 1)
            if i + 1 < NSUB:
                load_v(i + 1)
        ev_oacc_last[L] = evOA[(NSUB - 1, TJ - 1, 3)]
        ev_pe_flash_last[L] = evPV[(NSUB - 1, TJ - 1, 3)]
        return c("GPS")

    def emit_tail(L, ev_gps_flash):
        S.add("vector", lambda e: e.reciprocal(rlv[:, :], lrun[:, :]))
        S.add("vector", lambda e: e.drain(), inc="DVE")
        pe0 = c("PE")
        act0 = c("ACT")
        evN = {}
        evTPlast = {-2: pe0, -1: pe0}
        evE = []
        gi = 0
        for j in range(TJ):
            S.add("vector",
                  lambda e, j=j: e.tensor_scalar(attnj[:, j % 2, :], oacc[:, j, :],
                                                 rlv[:, j:j + 1], None, ALU.mult),
                  waits=[("PE", evTPlast[j - 2])], inc="DVE")
            evN[j] = c("DVE")
            for g in range(4):
                half = gi % 2
                def tp(e, j=j, g=g, half=half):
                    last = None
                    for k in range(4):
                        dc = g * 4 + k
                        last = e.transpose(psTs_k[half][:, k, :],
                                           attnj[:, j % 2, dc * 128:(dc + 1) * 128],
                                           sidb[:, :])
                    return last
                w = [("DVE", evN[j])]
                w.append(("ACT", evE[gi - 2] if gi >= 2 else act0))
                S.add("tensor", tp, waits=w, inc="PE")
                evTPlast[j] = c("PE")
                S.add("scalar",
                      lambda e, j=j, g=g, half=half: e.copy(
                          attnT[:, g * 4:(g + 1) * 4, j * 128:(j + 1) * 128],
                          psTs_k[half][:, :, :]),
                      waits=[("PE", evTPlast[j])], inc="ACT")
                evE.append(c("ACT"))
                gi += 1
        ev_norm_all = evN[TJ - 1]
        for s4 in range(4):
            prev_proj_mm[s4] = max(prev_proj_mm[s4], ev_pe_flash_last[L])
        ev_omm, ev_oact = emit_projT(L, woc, sbo, xT, attnT,
                                     extra_first=[("DVE", ev_norm_all)])
        tail_omm[0] = ev_omm
        if L == NL - 1:
            evEv = []
            gi = 0
            for dc in range(DC):
                for g in range(2):
                    half = gi % 2
                    def tp(e, dc=dc, g=g, half=half):
                        last = None
                        for k in range(4):
                            t = g * 4 + k
                            last = e.transpose(psTs_k[half][:, k, :],
                                               xT[:, dc, t * 128:(t + 1) * 128],
                                               sidb[:, :])
                        return last
                    w = [("ACT", ev_oact)]
                    if gi >= 2:
                        w.append(("DVE", evEv[gi - 2]))
                    S.add("tensor", tp, waits=w, inc="PE")
                    evt = c("PE")
                    S.add("vector",
                          lambda e, dc=dc, g=g, half=half: e.tensor_add(
                              x0rm[:, g * 4:(g + 1) * 4, dc * 128:(dc + 1) * 128],
                              x0rm[:, g * 4:(g + 1) * 4, dc * 128:(dc + 1) * 128],
                              psTs_k[half][:, :, :]),
                          waits=[("PE", evt)], inc="DVE")
                    evEv.append(c("DVE"))
                    gi += 1
            S.add("vector", lambda e: e.drain(), inc="DVE")
        return c("DVE"), ev_norm_all

    ev_resid = 0
    ev_norm_prev = [0]
    tail_omm = [0]
    for L in range(NL):
        if L == 0:
            ev_ln1 = emit_ln(lambda t: x0rm[:, t, :],
                             {0: [("LD_X0", 16)], 4: [("LD_X1", 16)]},
                             {0: [("LD_X0", 16), ("LD_INIT", ev_x0)],
                              4: [("LD_X1", 16)]})
            emit_rm_to_ct(ln_row, xT, ev_ln1,
                          extra0=[("LD_INIT", ev_x0)])

        S.add("gpsimd", lambda e: e.memset(negm[:, 1, :], NEG_BIG),
              waits=[("ACT", c("ACT"))])
        S.add("gpsimd", lambda e: e.drain(), inc="GPS")

        if L == 0:
            emit_wprep()
        for s4 in range(4):
            prev_proj_mm[s4] = max(prev_proj_mm[s4], ev_pe_flash_last[L - 1])
        kx = ([("DVE", ev_norm_prev[0])] if L > 0 else
              [("DVE", ev_wprep_dve[0]), ("ACT", ev_wprep_act[0])])
        ev_kmm, ev_kact = emit_projT(L, wkc, sbk, kTst, xT, extra_first=kx)
        buf = L % 2
        evst_k = []
        for s2 in range(2):
            dma("gpsimd", "ST_K", kv_in_v[s2].rearrange("p (a b) -> p a b", b=512),
                kTst[:, :, s2 * 512:(s2 + 1) * 512],
                waits=[("ACT", ev_kact)])
            evst_k.append(c("ST_K"))

        def issue_kcc(s2):
            S.add("gpsimd",
                  lambda e, s2=s2, buf=buf: e.collective_compute(
                      "AllGather", ALU.bypass, replica_groups=[list(range(NCORE))],
                      ins=[kv_in[s2 * 128:(s2 + 1) * 128, :].opt()],
                      outs=[g_k[s2][buf].opt()]),
                  waits=[("ST_K", evst_k[s2])], inc="CCK%d" % s2)
            ev_cc[(L, "k", s2)] = ("CCK%d" % s2, c("CCK%d" % s2))
        issue_kcc(0)

        dma("sync", "LD_WV", bvb[:, :], bvn[L:L + 1, :],
            waits=[("PE", ev_pe_flash_last[L - 1])])
        wv_v = wvn[L].rearrange("(a p) d -> p a d", p=128)
        w_wv = [("PE", ev_pe_flash_last[L - 1])]
        dma("sync", "LD_WV", wvlo8[:, :, :], wv_v[:, 0:8, :], waits=w_wv)
        w_wvh = ([("PE", ev_wprep_pe[0]), ("ST_W", ev_wprep_stg[0])] if L == 0
                 else [("PE", tail_omm[0])])
        dma("sync", "LD_WV", wvhi8[:, :, :], wv_v[:, 8:16, :], waits=w_wvh)
        ev_wv = c("LD_WV")
        ev_vcopy = {}
        st_v_hist = []
        for m in range(TJ):
            base = (m % 2) * 4
            for db in range(4):
                def vmm_db(e, m=m, db=db):
                    last = None
                    for dcin in range(DC):
                        src = wvlo8 if dcin < 8 else wvhi8
                        last = e.matmul(psPV[:, db * 512:(db + 1) * 512],
                                        xT[:, dcin, m * 128:(m + 1) * 128],
                                        src[:, dcin % 8, db * 512:(db + 1) * 512],
                                        start=(dcin == 0), stop=False)
                    last = e.matmul(psPV[:, db * 512:(db + 1) * 512],
                                    sonb[:, :], bvb[0:1, db * 512:(db + 1) * 512],
                                    start=False, stop=True)
                    return last
                w = []
                if m == 0 and db == 0:
                    w.append(("LD_WV", ev_wv))
                    w.append(("DVE", ev_oacc_last[L - 1]))
                    w.append(("ACT", c("ACT")))
                elif (m - 1, db) in ev_vcopy:
                    w.append(("ACT", ev_vcopy[(m - 1, db)]))
                S.add("tensor", vmm_db, waits=w, inc="PE")
                ev_vmm = c("PE")
                wcp = [("PE", ev_vmm)]
                if db == 0 and m >= 2:
                    wcp.append(("ST_V", st_v_hist[m - 2]))
                S.add("scalar",
                      lambda e, db=db, base=base: e.copy(
                          pex[:, base + db, :], psPV[:, db * 512:(db + 1) * 512]),
                      waits=wcp, inc="ACT")
                ev_vcopy[(m, db)] = c("ACT")
            eva = c("ACT")
            dma("gpsimd", "ST_V",
                kv_in_v[2 + m // 4][:, (m % 4) * 2048:(m % 4 + 1) * 2048],
                pex[:, base:base + 4, :], waits=[("ACT", eva)])
            st_v_hist.append(c("ST_V"))
            if m == 3 or m == 7:
                s2 = m // 4
                evst = c("ST_V")
                S.add("gpsimd",
                      lambda e, s2=s2, buf=buf: e.collective_compute(
                          "AllGather", ALU.bypass, replica_groups=[list(range(NCORE))],
                          ins=[kv_in[(2 + s2) * 128:(3 + s2) * 128, :].opt()],
                          outs=[g_v[s2][buf].opt()]),
                      waits=[("ST_V", evst)], inc="CCV%d" % s2)
                ev_cc[(L, "v", s2)] = ("CCV%d" % s2, c("CCV%d" % s2))
                if m == 3:
                    issue_kcc(1)
        ev_vlast[L] = c("PE")
        ev_vlast_stg[L] = c("ST_V")
        ev_stores_kv[L] = c("ST_K")
        for s3 in ({0: [0], 1: [1], 2: [2], 3: [3]}.get(L, [])):
            S.add("gpsimd",
                  lambda e, s3=s3: e.collective_compute(
                      "AllGather", ALU.bypass, replica_groups=[list(range(NCORE))],
                      ins=[w_in[s3 * 1024:(s3 + 1) * 1024, :].opt()],
                      outs=[w_alls[s3].opt()]),
                  waits=[("ST_W", ev_wprep_stg[0])], inc="CCW")
        ev_cc["Wall"] = ("CCW", 4)

        ev_qmm, ev_qact = emit_projT(L, wqc, sbq, qT, xT)
        ev_qmm_flash[0] = ev_qmm
        S.add("vector", lambda e: e.memset(oacc[:, :, :], 0.0),
              waits=[("PE", ev_qmm), ("ST_K", c("ST_K"))])
        S.add("vector", lambda e: e.drain())
        ev_gps_flash = emit_flash(L, ev_qact)
        ev_resid, ev_na = emit_tail(L, ev_gps_flash)
        ev_norm_prev[0] = ev_na

    ev_ln2 = emit_ln(lambda t: x0rm[:, t, :], [("DVE", ev_resid)],
                     [("PE", c("PE")), ("DVE", ev_resid)])
    emit_rm_to_ct(lambda t: x0rm[:, t, :], xT, ev_resid)
    act0 = c("ACT")
    evX = []
    gi = 0
    for dc in range(DC):
        for g in range(2):
            half = gi % 2
            def tp(e, dc=dc, g=g, half=half):
                last = None
                for k in range(4):
                    t = g * 4 + k
                    last = e.transpose(psTs_k[half][:, k, :],
                                       ln_row(t)[:, dc * 128:(dc + 1) * 128], sidb[:, :])
                return last
            w = [("DVE", ev_ln2), ("ACT", evX[gi - 2] if gi >= 2 else act0)]
            S.add("tensor", tp, waits=w, inc="PE")
            evt = c("PE")
            S.add("scalar",
                  lambda e, dc=dc, g=g, half=half: e.copy(
                      XT[:, dc, g * 512:(g + 1) * 512], psTs[half][:, :]),
                  waits=[("PE", evt)], inc="ACT")
            evX.append(c("ACT"))
            gi += 1
    ev_xtf = c("ACT")

    dve_ffn0 = c("DVE")
    pe_ffn0 = c("PE")
    prev_act_gate = ev_xtf
    evd_hist = []
    evB_all = []
    ev_stg_act_prev = None
    st_out_hist = []
    for half in range(2):
        evA = {}
        evGel = {}
        for ncx in range(NC2):
            r, lc = ncx // 8, ncx % 8
            slot = ncx % 2
            fsem = "LD_F0%d" % slot
            wld = [ev_cc["Wall"]]
            wld.append(("PE", evA[ncx - 2] if ncx >= 2 else c("PE")))
            for hl in range(2):
                dma("sync", fsem, w0buf[:, slot, hl, :], w_all_v[hl][r, lc, :, :],
                    waits=wld if hl == 0 else [])
            evw = (fsem, c(fsem))
            bank = ncx % 2
            def amm(e, slot=slot, bank=bank, half=half):
                last = None
                w0v = w0buf.ap().rearrange("p s l (a b) -> p s l a b", a=DC)
                for dc in range(DC):
                    for hl in range(2):
                        last = e.matmul(psS[bank][:, :],
                                        w0v[:, slot, hl, dc, :],
                                        XT[:, dc, half * 512:(half + 1) * 512],
                                        start=(dc == 0 and hl == 0),
                                        stop=(dc == DC - 1 and hl == 1))
                return last
            w = [evw, ("ACT", prev_act_gate if ncx < 2 else evGel[ncx - 2])]
            S.add("tensor", amm, waits=w, inc="PE")
            evA[ncx] = c("PE")
            dtgt = d16a if ncx < 32 else d16b
            S.add("scalar",
                  lambda e, ncx=ncx, bank=bank, dtgt=dtgt: e.activation(
                      dtgt[:, ncx % 32, :], psS[bank][:, :], AF.Gelu,
                      bias=sb0[:, ncx:ncx + 1], scale=1.0),
                  waits=[("PE", evA[ncx])], inc="ACT")
            evGel[ncx] = c("ACT")
        ev_gelu = c("ACT")
        for pas in range(4):
            for ncx in range(NC2):
                r, lc = ncx // 8, ncx % 8
                g = len(evB_all)
                slot = g % 8
                fsem = "LD_F1%d" % slot
                wld = [("PE", evB_all[g - 8] if g >= 8 else pe_ffn0),
                       ev_cc["Wall"]]
                for hl in range(2):
                    dma("sync", fsem, w1buf[:, slot, hl, :],
                        w_all_v[2 + hl][r, lc, :, pas * 512:(pas + 1) * 512],
                        waits=wld if hl == 0 else [])
                evw = (fsem, c(fsem))
                dsrc = d16a if ncx < 32 else d16b
                def bmm(e, ncx=ncx, slot=slot, dsrc=dsrc):
                    last = None
                    w1v = w1buf.ap().rearrange("p s l (a b) -> p s l a b", a=4)
                    for hl in range(2):
                        for hc in range(4):
                            last = e.matmul(psPV[:, hc * 512:(hc + 1) * 512],
                                            w1v[:, slot, hl, hc, :],
                                            dsrc[:, ncx % 32, :],
                                            start=(ncx == 0 and hl == 0),
                                            stop=(ncx == NC2 - 1 and hl == 1))
                    return last
                w = [evw, ("ACT", ev_gelu)]
                if ncx == 0:
                    if ev_stg_act_prev is not None:
                        w.append(("ACT", ev_stg_act_prev))
                    else:
                        w.append(("DVE", dve_ffn0))
                        w.append(("ACT", c("ACT")))
                S.add("tensor", bmm, waits=w, inc="PE")
                evB_all.append(c("PE"))
            evB = evB_all[-1]
            for hc in range(4):
                hch = pas * 4 + hc
                we = [("PE", evB)]
                if len(evd_hist) >= 4:
                    we.append(("DVE", evd_hist[-4]))
                S.add("scalar",
                      lambda e, hc=hc, hch=hch: e.activation(
                          e16s[:, hc, :], psPV[:, hc * 512:(hc + 1) * 512],
                          AF.Identity, bias=sb1[:, hch:hch + 1], scale=1.0),
                      waits=we, inc="ACT")
                eva = c("ACT")
                wadd = [("ACT", eva)]
                if len(st_out_hist) >= 4:
                    wadd.append(("ST_OUT", st_out_hist[-4]))
                S.add("vector",
                      lambda e, hc=hc, hch=hch, half=half: e.tensor_add(
                          outb[:, hc, :],
                          xT[:, hch, half * 512:(half + 1) * 512],
                          e16s[:, hc, :]),
                      waits=wadd, inc="DVE")
                evd = c("DVE")
                evd_hist.append(evd)
                dma("gpsimd", "ST_OUT",
                    y_t.rearrange("(a p) r -> a p r", p=128)[pas * 4 + hc]
                       [:, half * 512:(half + 1) * 512],
                    outb[:, hc, :], waits=[("DVE", evd)])
                st_out_hist.append(c("ST_OUT"))
            ev_stg_act_prev = c("ACT")
        prev_act_gate = c("ACT")
    ev_out = c("ST_OUT")
    S.add("gpsimd", None, waits=[("ST_OUT", ev_out)])

    from contextlib import ExitStack
    sems = {}
    es = ExitStack()
    with es:
        for sname in ("LD_INIT", "LD_X0", "LD_X1",
                      "LD_PW0", "LD_PW1", "LD_PW2", "LD_PW3",
                      "LD_WV", "LD_WCH0", "LD_WCH1",
                      "LD_K", "LD_V0", "LD_V1", "LD_F00", "LD_F01",
                      "LD_F10", "LD_F11", "LD_F12", "LD_F13",
                      "LD_F14", "LD_F15", "LD_F16", "LD_F17",
                      "ST_K", "ST_V", "ST_W", "ST_OUT",
                      "CCK0", "CCK1", "CCV0", "CCV1", "CCW",
                      "PE", "ACT", "DVE", "GPS"):
            sems[sname] = es.enter_context(nc.semaphore(sname))
        blk = es.enter_context(nc.Block())
        S.emit(blk, sems)
    n_items = {e: len(S.items[e]) for e in ENGS}
    return nc, n_items


_NC_CACHE = None


def kernel(**inputs):
    global _NC_CACHE
    xs = np.asarray(inputs["L0_attn_x"])
    Wq = np.asarray(inputs["L0_attn_Wq"])
    bq = np.asarray(inputs["L0_attn_bq"])
    Wk = np.asarray(inputs["L0_attn_Wk"])
    bk = np.asarray(inputs["L0_attn_bk"])
    Wv = np.asarray(inputs["L0_attn_Wv"])
    bv = np.asarray(inputs["L0_attn_bv"])
    Wo = np.asarray(inputs["L0_attn_Wo"])
    bo = np.asarray(inputs["L0_attn_bo"])
    W0 = np.asarray(inputs["L0_ffn_W0"], dtype=np.float32)
    bias0 = np.asarray(inputs["L0_ffn_bias0"], dtype=np.float32)
    W1 = np.asarray(inputs["L0_ffn_W1"], dtype=np.float32)
    bias1 = np.asarray(inputs["L0_ffn_bias1"], dtype=np.float32)

    x2 = np.ascontiguousarray(xs.reshape(SEQ, D))

    def colchunk(W):
        return np.ascontiguousarray(
            W.reshape(NL, DC, 128, DC, 128).transpose(0, 3, 2, 1, 4))

    def biasrc(b):
        return np.ascontiguousarray(b.reshape(NL, DC, 128).transpose(2, 0, 1))

    common = {
        "wqc": colchunk(Wq), "wkc": colchunk(Wk), "woc": colchunk(Wo),
        "wvn": np.ascontiguousarray(Wv),
        "bqr": biasrc(bq), "bkr": biasrc(bk), "bor": biasrc(bo),
        "bvn": np.ascontiguousarray(bv),
        "b0r": np.ascontiguousarray(bias0.reshape(NC2, 128).T),
        "b1r": np.ascontiguousarray(bias1.reshape(DC, 128).T),
        "identb": np.eye(128, dtype=ml_dtypes.bfloat16),
        "identf": np.eye(128, dtype=np.float32),
        "onesb": np.ones((1, 128), dtype=ml_dtypes.bfloat16),
    }
    in_maps = []
    for cid in range(NCORE):
        m = dict(common)
        m["x"] = np.ascontiguousarray(x2[cid * RPC:(cid + 1) * RPC])
        m["w0s"] = np.ascontiguousarray(W0[cid * RPC:(cid + 1) * RPC])
        m["w1s"] = np.ascontiguousarray(W1[:, cid * RPC:(cid + 1) * RPC])
        in_maps.append(m)

    if _NC_CACHE is None:
        _NC_CACHE = build_nc()[0]
    res = run_bass_kernel_spmd(_NC_CACHE, in_maps, core_ids=list(range(NCORE)))
    parts = [np.asarray(res.results[cid]["y_t"]).T for cid in range(NCORE)]
    out = np.concatenate(parts, axis=0).reshape(1, SEQ, D)
    return out


if __name__ == "__main__":
    nc, n = build_nc()
    print("built ok; items:", n, "total:", sum(n.values()))


# version 20
# speedup vs baseline: 1.0842x; 1.0390x over previous
import math
import numpy as np
import ml_dtypes
import concourse.bass as bass
import concourse.mybir as mybir
from concourse.bass_utils import run_bass_kernel_spmd

F32 = mybir.dt.float32
BF16 = mybir.dt.bfloat16
F16 = mybir.dt.float16
AX = mybir.AxisListType
ALU = mybir.AluOpType
AF = mybir.ActivationFunctionType

NCORE = 8
SEQ = 8192
D = 2048
RPC = SEQ // NCORE
NL = 4
NFF = 8192
DC = D // 128
TJ = RPC // 128
NSUB = 16
NC2 = NFF // 128
SCALE = 1.0 / math.sqrt(D)
NEG_BIG = 3.0e38

ENGS = ("sync", "tensor", "vector", "scalar", "gpsimd")


class Sched:

    def __init__(self):
        self.items = {e: [] for e in ENGS}
        self.cnt = {}

    def c(self, sem):
        return self.cnt.get(sem, 0)

    def add(self, eng, fn, waits=(), inc=None, inck=1):
        val = None
        if inc is not None:
            self.cnt[inc] = self.cnt.get(inc, 0) + inck
            val = self.cnt[inc]
        ws = [(s, v) for (s, v) in waits if v and v > 0]
        self.items[eng].append((fn, ws, inc, inck))
        return val

    def emit(self, blk, sems):
        def mk(name):
            def run(eng):
                seen = {}
                for fn, waits, inc, inck in self.items[name]:
                    for (s, v) in waits:
                        if seen.get(s, 0) < v:
                            eng.wait_ge(sems[s], v)
                            seen[s] = v
                    inst = fn(eng) if fn is not None else None
                    if inc is not None:
                        if inst is not None:
                            inst.then_inc(sems[inc], inck)
                        else:
                            eng.sem_inc(sems[inc], inck)
            return run
        blk.sync(mk("sync"))
        blk.tensor(mk("tensor"))
        blk.vector(mk("vector"))
        blk.scalar(mk("scalar"))
        blk.gpsimd(mk("gpsimd"))


def build_nc():
    nc = bass.Bass("TRN2", target_bir_lowering=False)

    def ein(name, shape, dt):
        return nc.dram_tensor(name, shape, dt, kind="ExternalInput").ap()

    x_in = ein("x", [RPC, D], BF16)
    wqc = ein("wqc", [NL, DC, 128, DC, 128], BF16)
    wkc = ein("wkc", [NL, DC, 128, DC, 128], BF16)
    woc = ein("woc", [NL, DC, 128, DC, 128], BF16)
    wvn = ein("wvn", [NL, D, D], BF16)
    bqr = ein("bqr", [128, NL, DC], BF16)
    bkr = ein("bkr", [128, NL, DC], BF16)
    bor = ein("bor", [128, NL, DC], BF16)
    bvn = ein("bvn", [NL, D], BF16)
    w0s = ein("w0s", [RPC, D], F32)
    w1s = ein("w1s", [D, RPC], F32)
    b0r = ein("b0r", [128, NC2], F32)
    b1r = ein("b1r", [128, DC], F32)
    identb = ein("identb", [128, 128], BF16)
    identf = ein("identf", [128, 128], F32)
    onesb = ein("onesb", [1, 128], BF16)

    y_t = nc.dram_tensor("y_t", [D, RPC], F32, kind="ExternalOutput").ap()

    kv_in = nc.dram_tensor("kv_in", [4 * 128, SEQ], BF16).ap()
    g_k = [[nc.dram_tensor(f"g_k{s}{b}", [NCORE * 128, SEQ], BF16,
                           addr_space="Shared").ap()
            for b in range(2)] for s in range(2)]
    g_v = [[nc.dram_tensor(f"g_v{s}{b}", [NCORE * 128, SEQ], BF16,
                           addr_space="Shared").ap()
            for b in range(2)] for s in range(2)]
    w_in = nc.dram_tensor("w_in", [4 * 8 * 128, D], F16).ap()
    w_alls = [nc.dram_tensor(f"w_all{s}", [NCORE * 8 * 128, D], F16,
                             addr_space="Shared").ap()
              for s in range(4)]

    kv_in_v = kv_in.rearrange("(a p) n -> a p n", p=128)
    g_k_v = [[g_k[s][b].rearrange("(r p) n -> r p n", p=128) for b in range(2)] for s in range(2)]
    g_v_v = [[g_v[s][b].rearrange("(r p) n -> r p n", p=128) for b in range(2)] for s in range(2)]
    w_in_v = w_in.rearrange("(s l p) h -> s l p h", l=8, p=128)
    w_in_pv = w_in.rearrange("(s l p) h -> s p l h", l=8, p=128)
    w_all_v = [w_alls[s].rearrange("(r l p) h -> r l p h", l=8, p=128)
               for s in range(4)]

    def sb(name, shape, dt, off):
        return nc.alloc_sbuf_tensor_at(name, shape, dt, offset=off)

    BASE = 16512
    x0rm = sb("x0rm", [128, TJ, D], BF16, BASE)
    XT = sb("XT", [128, DC, RPC], F16, BASE)
    RA = BASE + 32768
    lnout = sb("lnout", [128, TJ, D], BF16, RA)
    kTst = sb("kTst", [128, DC, RPC], BF16, RA)
    d16b = sb("d16b", [128, 32, 512], F16, RA)
    oacc = sb("oacc", [128, TJ, D], F32, RA)
    xT = sb("xT", [128, DC, RPC], BF16, RA + 32768)
    RB = RA + 65536
    qT = sb("qT", [128, DC, RPC], BF16, RB)
    attnT = sb("attnT", [128, DC, RPC], BF16, RB)
    wch = sb("wch", [128, D], F32, RB)
    w0stg = sb("w0stg", [128, 2, DC, 128], F16, RB + 8192)
    w1stg = sb("w1stg", [128, 2, 8, 4, 128], F16, RB + 16384)
    wvhi8 = sb("wvhi8", [128, 8, D], BF16, RB)
    w0buf = sb("w0buf", [128, 2, 2, DC * 128], F16, RB)
    w1buf = sb("w1buf", [128, 8, 2, 512], F16, RB + 16384)
    KV = RB + 32768
    kslot = sb("kslot", [128, DC, 512], BF16, KV)
    vslot = sb("vslot", [128, 4, D], BF16, KV + 16384)
    wvlo8 = sb("wvlo8", [128, 8, D], BF16, KV)
    d16a = sb("d16a", [128, 32, 512], F16, KV)
    SM = KV + 32768
    projw = sb("projw", [128, 4, DC, 128], BF16, SM)
    pex = sb("pex", [128, TJ, 512], BF16, SM + 16384)
    e16s = sb("e16s", [128, 4, 512], F16, SM + 16384)
    ptsb = sb("ptsb", [128, 2, 4, 128], BF16, SM + 24576)
    attnj = sb("attnj", [128, 2, D], BF16, SM + 26624)
    vslot2 = sb("vslot2", [128, 4, D], BF16, SM)
    wch2 = sb("wch2", [128, D], F32, SM + 26624)
    outb = sb("outb", [128, 4, 512], F32, SM + 26624)
    bvb = sb("bvb", [1, D], BF16, SM + 34816)
    sbq = sb("sbq", [128, NL, DC], BF16, SM + 38912)
    sbk = sb("sbk", [128, NL, DC], BF16, SM + 39040)
    sbo = sb("sbo", [128, NL, DC], BF16, SM + 39168)
    sb0 = sb("sb0", [128, NC2], F32, SM + 39296)
    sb1 = sb("sb1", [128, DC], F32, SM + 39552)
    sidb = sb("sidb", [128, 128], BF16, SM + 39616)
    sidf = sb("sidf", [128, 128], F32, SM + 39872)
    sonb = sb("sonb", [1, 128], BF16, SM + 40384)
    ST = SM + 40640
    negm = sb("negm", [128, 2, TJ], F32, ST)
    nmraw = sb("nmraw", [128, TJ], F32, ST + 64)
    dcol = sb("dcol", [128, TJ], F32, ST + 96)
    alpha = sb("alpha", [128, TJ], F32, ST + 128)
    lblk = sb("lblk", [128, TJ], F32, ST + 160)
    lrun = sb("lrun", [128, TJ], F32, ST + 192)
    rlv = sb("rlv", [128, TJ], F32, ST + 224)
    lnsum = sb("lnsum", [128, TJ], F32, ST + 256)
    lnsq = sb("lnsq", [128, TJ], F32, ST + 288)
    lnm = sb("lnm", [128, TJ], F32, ST + 320)
    lnr = sb("lnr", [128, TJ], F32, ST + 352)
    sbf = sb("sbf", [128, 4, 512], BF16, ST + 512)
    assert ST + 512 + 4096 <= 229376

    psS = [nc.alloc_psum_tensor("psS0", [128, 512], F32),
           nc.alloc_psum_tensor("psS1", [128, 512], F32)]
    psTs = [nc.alloc_psum_tensor("psT0", [128, 512], BF16),
            nc.alloc_psum_tensor("psT1", [128, 512], BF16)]
    psPV = nc.alloc_psum_tensor("psPV", [128, 2048], F32)
    psTs_k = [t.ap().rearrange("p (k d) -> p k d", k=4) for t in psTs]

    S = Sched()
    c = S.c

    def dma(eng, sem, out, in_, waits=()):
        return S.add(eng, lambda e, o=out, i=in_: e.dma_start(out=o, in_=i),
                     waits=waits, inc=sem, inck=16)

    x_in_v = x_in.rearrange("(t p) d -> p t d", p=128)
    for h in range(2):
        dma("sync", "LD_X%d" % h, x0rm[:, 4 * h:4 * h + 4, :],
            x_in_v[:, 4 * h:4 * h + 4, :])
    dma("sync", "LD_INIT", sbq[:, :, :], bqr)
    dma("sync", "LD_INIT", sbk[:, :, :], bkr)
    dma("sync", "LD_INIT", sbo[:, :, :], bor)
    dma("sync", "LD_INIT", sb0[:, :], b0r)
    dma("sync", "LD_INIT", sb1[:, :], b1r)
    dma("sync", "LD_INIT", sidb[:, :], identb)
    dma("sync", "LD_INIT", sidf[:, :], identf)
    dma("sync", "LD_INIT", sonb[:, :], onesb)
    ev_x0 = c("LD_INIT")

    S.add("gpsimd", lambda e: e.memset(negm[:, :, :], NEG_BIG))
    S.add("gpsimd", lambda e: e.memset(lrun[:, :], 0.0))
    S.add("gpsimd", lambda e: e.drain(), inc="GPS")

    def ln_row(t):
        return lnout[:, t, :]

    def emit_ln(src_row, wait_dve, wait_act):
        wd = wait_dve if isinstance(wait_dve, dict) else {0: wait_dve}
        wa = wait_act if isinstance(wait_act, dict) else {0: wait_act}
        for t in range(TJ):
            S.add("vector",
                  lambda e, t=t: e.reduce_sum(lnsum[:, t:t + 1], src_row(t), AX.X),
                  waits=wd.get(t, []), inc="DVE")
        ev_sums = c("DVE")
        for t in range(TJ):
            S.add("scalar",
                  lambda e, t=t: e.activation(ln_row(t), src_row(t), AF.Square,
                                              bias=0.0, scale=1.0,
                                              accum_out=lnsq[:, t:t + 1]),
                  waits=wa.get(t, []), inc="ACT")
        ev_sq = c("ACT")
        S.add("vector", lambda e: e.tensor_scalar(lnm[:, :], lnsum[:, :], 1.0 / D, None, ALU.mult),
              waits=[("DVE", ev_sums)])
        S.add("vector", lambda e: e.drain())
        S.add("vector", lambda e: e.tensor_scalar(lnr[:, :], lnsq[:, :], 1.0 / D, None, ALU.mult),
              waits=[("ACT", ev_sq)])
        S.add("vector", lambda e: e.drain())
        S.add("vector", lambda e: e.tensor_mul(lnsq[:, :], lnm[:, :], lnm[:, :]))
        S.add("vector", lambda e: e.drain())
        S.add("vector", lambda e: e.tensor_sub(lnr[:, :], lnr[:, :], lnsq[:, :]))
        S.add("vector", lambda e: e.drain())
        S.add("vector", lambda e: e.tensor_scalar(lnr[:, :], lnr[:, :], 1e-5, None, ALU.add))
        S.add("vector", lambda e: e.drain(), inc="DVE")
        ev_var = c("DVE")
        S.add("scalar", lambda e: e.activation(lnr[:, :], lnr[:, :], AF.Ln, bias=0.0, scale=1.0),
              waits=[("DVE", ev_var)])
        S.add("scalar", lambda e: e.drain())
        S.add("scalar", lambda e: e.activation(lnr[:, :], lnr[:, :], AF.Exp, bias=0.0, scale=-0.5))
        S.add("scalar", lambda e: e.drain(), inc="ACT")
        ev_rstd = c("ACT")
        for t in range(TJ):
            S.add("vector",
                  lambda e, t=t: e.tensor_scalar(ln_row(t), src_row(t),
                                                 lnm[:, t:t + 1], lnr[:, t:t + 1],
                                                 ALU.subtract, ALU.mult),
                  waits=[("ACT", ev_rstd)] if t == 0 else [], inc="DVE")
        return c("DVE")

    def emit_rm_to_ct(src_row, dst_ct, wait_dve_val, extra0=()):
        act0 = c("ACT")
        evs = []
        gidx = 0
        for dc in range(DC):
            for g in range(2):
                half = gidx % 2
                def tp(e, dc=dc, g=g, half=half):
                    last = None
                    for k in range(4):
                        t = g * 4 + k
                        last = e.transpose(psTs_k[half][:, k, :],
                                           src_row(t)[:, dc * 128:(dc + 1) * 128],
                                           sidb[:, :])
                    return last
                w = []
                if gidx < 2:
                    w = [("DVE", wait_dve_val), ("ACT", act0)] + list(extra0)
                else:
                    w = [("ACT", evs[gidx - 2])]
                S.add("tensor", tp, waits=w, inc="PE")
                evt = c("PE")
                S.add("scalar",
                      lambda e, dc=dc, g=g, half=half: e.copy(
                          dst_ct[:, dc, g * 512:(g + 1) * 512], psTs[half][:, :]),
                      waits=[("PE", evt)], inc="ACT")
                evs.append(c("ACT"))
                gidx += 1
        return c("PE"), c("ACT")

    prev_proj_mm = [0, 0, 0, 0]

    def emit_projT(L, wsrc, bias_sb, dst, rhs_src, extra_first=()):
        act0 = c("ACT")
        ev_mm = []
        ev_ev = []
        for doc in range(DC):
            slot = doc % 4
            psem = "LD_PW%d" % slot
            wld = [("PE", ev_mm[2 * doc - 7] if doc >= 4 else prev_proj_mm[slot])]
            dma("sync", psem, projw[:, slot, :, :].rearrange("p a b -> p (a b)"),
                wsrc[L, doc].rearrange("p a b -> p (a b)"), waits=wld)
            evw = (psem, c(psem))
            for half in range(2):
                n = 2 * doc + half
                bank = n % 2
                def mm(e, doc=doc, half=half, slot=slot, bank=bank):
                    last = None
                    for dcin in range(DC):
                        last = e.matmul(psS[bank][:, :],
                                        projw[:, slot, dcin, :],
                                        rhs_src[:, dcin, half * 512:(half + 1) * 512],
                                        start=(dcin == 0), stop=(dcin == DC - 1))
                    return last
                w = [evw]
                w.append(("ACT", ev_ev[n - 2] if n >= 2 else act0))
                if n < 2:
                    w.extend(extra_first)
                S.add("tensor", mm, waits=w, inc="PE")
                ev_mm.append(c("PE"))
                S.add("scalar",
                      lambda e, doc=doc, half=half, bank=bank: e.activation(
                          dst[:, doc, half * 512:(half + 1) * 512], psS[bank][:, :],
                          AF.Identity, bias=bias_sb[:, L, doc:doc + 1], scale=1.0),
                      waits=[("PE", ev_mm[-1])], inc="ACT")
                ev_ev.append(c("ACT"))
        prev_proj_mm[0] = ev_mm[-7]
        prev_proj_mm[1] = ev_mm[-5]
        prev_proj_mm[2] = ev_mm[-3]
        prev_proj_mm[3] = ev_mm[-1]
        return ev_mm[-1], ev_ev[-1]

    ev_wprep_pe = [0]
    ev_wprep_dve = [0]
    ev_wprep_act = [0]
    ev_wprep_stg = [0]

    def emit_wprep():
        act0 = c("ACT")
        wchs = [wch, wch2]
        bank_gate = {0: (0, 0), 1: (0, 0)}
        evtp_rt = {-2: 0, -1: 0}
        for rt in range(8):
            csem = "LD_WCH%d" % (rt % 2)
            dma("sync", csem, wchs[rt % 2][:, :],
                w0s.rearrange("(a p) d -> a p d", p=128)[rt],
                waits=[("PE", evtp_rt[rt - 2])])
            evl = (csem, c(csem))
            for g in range(4):
                bank = g % 2
                def tp(e, rt=rt, g=g, bank=bank):
                    psv = psS[bank].ap().rearrange("p (k d) -> p k d", k=4)
                    last = None
                    for k in range(4):
                        dc = g * 4 + k
                        last = e.transpose(psv[:, k, :],
                                           wchs[rt % 2][:, dc * 128:(dc + 1) * 128],
                                           sidf[:, :])
                    return last
                S.add("tensor", tp, waits=[evl, ("DVE", bank_gate[bank][1]),
                                           ("ACT", bank_gate[bank][0] or act0)], inc="PE")
                evtp_rt[rt] = c("PE")
                evt = c("PE")
                S.add("scalar",
                      lambda e, g=g, bank=bank: e.copy(
                          w0stg[:, 0, g * 4:(g + 1) * 4, :].rearrange("p a b -> p (a b)"),
                          psS[bank][:, :]),
                      waits=[("PE", evt), ("ST_W", c("ST_W"))], inc="ACT")
                evc = c("ACT")
                S.add("vector",
                      lambda e, g=g, bank=bank: e.tensor_sub(
                          w0stg[:, 1, g * 4:(g + 1) * 4, :].rearrange("p a b -> p (a b)"),
                          psS[bank][:, :],
                          w0stg[:, 0, g * 4:(g + 1) * 4, :].rearrange("p a b -> p (a b)")),
                      waits=[("ACT", evc)], inc="DVE")
                bank_gate[bank] = (evc, c("DVE"))
            evd = c("DVE")
            for hl in range(2):
                dma("gpsimd", "ST_W", w_in_v[hl, rt, :, :],
                    w0stg[:, hl, :, :].rearrange("p a b -> p (a b)"),
                    waits=[("DVE", evd)])
        evtp_ht = {-2: evtp_rt[6], -1: evtp_rt[7]}
        for htg in range(4):
            for hh in range(4):
                ht = htg * 4 + hh
                csem = "LD_WCH%d" % (ht % 2)
                dma("sync", csem, wchs[ht % 2][:, 0:1024],
                    w1s.rearrange("(a p) n -> a p n", p=128)[ht],
                    waits=[("PE", evtp_ht[ht - 2])])
                evl = (csem, c(csem))
                for g in range(2):
                    bank = g % 2
                    def tp(e, ht=ht, g=g, bank=bank):
                        psv = psS[bank].ap().rearrange("p (k d) -> p k d", k=4)
                        last = None
                        for k in range(4):
                            ns = g * 4 + k
                            last = e.transpose(psv[:, k, :],
                                               wchs[ht % 2][:, ns * 128:(ns + 1) * 128],
                                               sidf[:, :])
                        return last
                    S.add("tensor", tp, waits=[evl, ("DVE", bank_gate[bank][1]),
                                               ("ACT", bank_gate[bank][0])], inc="PE")
                    evtp_ht[ht] = c("PE")
                    evt = c("PE")
                    S.add("scalar",
                          lambda e, g=g, hh=hh, bank=bank: e.copy(
                              w1stg[:, 0, g * 4:(g + 1) * 4, hh, :],
                              psS[bank].ap().rearrange("p (k d) -> p k d", k=4)),
                          waits=[("PE", evt), ("ST_W", c("ST_W"))], inc="ACT")
                    evc = c("ACT")
                    S.add("vector",
                          lambda e, g=g, hh=hh, bank=bank: e.tensor_sub(
                              w1stg[:, 1, g * 4:(g + 1) * 4, hh, :],
                              psS[bank].ap().rearrange("p (k d) -> p k d", k=4),
                              w1stg[:, 0, g * 4:(g + 1) * 4, hh, :]),
                          waits=[("ACT", evc)], inc="DVE")
                    bank_gate[bank] = (evc, c("DVE"))
            evd = c("DVE")
            for hl in range(2):
                dma("gpsimd", "ST_W", w_in_pv[2 + hl][:, :, htg * 512:(htg + 1) * 512],
                    w1stg[:, hl, :, :, :].rearrange("p a b c -> p a (b c)"),
                    waits=[("DVE", evd)])
        ev_wprep_pe[0] = c("PE")
        ev_wprep_dve[0] = c("DVE")
        ev_wprep_act[0] = c("ACT")
        ev_wprep_stg[0] = c("ST_W")

    ev_cc = {}
    ev_vlast = {}
    ev_vlast_stg = {}
    ev_stores_kv = {}
    ev_oacc_last = {-1: 0}
    ev_pe_flash_last = {-1: 0}

    ev_qmm_flash = [0]

    def emit_flash(L, ev_qact):
        buf = L % 2
        evS, evSC, evST, evLU, evP, evT, evPTE, evPV, evOA = ({} for _ in range(9))
        evLD, evLDV = {}, {}

        def rs(i):
            return i % 8, i // 8

        def load_k(i):
            r, s = rs(i)
            w = [ev_cc[(L, "k", s)]]
            w.append(("PE", ev_vlast[L] if i == 0 else evS[(i - 1, TJ - 1)]))
            dma("sync", "LD_K", kslot.ap().rearrange("p a b -> p (a b)"),
                g_k_v[s][buf][r], waits=w)
            evLD[i] = c("LD_K")

        def load_v(i):
            r, s = rs(i)
            vs = [vslot, vslot2][i % 2]
            w = [ev_cc[(L, "v", s)]]
            if i == 0:
                w.append(("PE", ev_vlast[L]))
            elif i == 1:
                w.append(("PE", ev_qmm_flash[0]))
            else:
                w.append(("PE", evPV[(i - 2, TJ - 1, 3)]))
            vsem = "LD_V%d" % (i % 2)
            dma("sync" if i < 2 else "scalar", vsem,
                vs.ap().rearrange("p a b -> p (a b)"),
                g_v_v[s][buf][r], waits=w)
            evLDV[i] = (vsem, c(vsem))

        def emit_pv(i, j):
            idx = i * TJ + j
            vs = [vslot, vslot2][i % 2]
            for db in range(4):
                def pv(e, j=j, db=db, vs=vs):
                    last = None
                    for kc in range(4):
                        last = e.matmul(psPV[:, db * 512:(db + 1) * 512],
                                        ptsb[:, j % 2, kc, :],
                                        vs[:, kc, db * 512:(db + 1) * 512],
                                        start=(kc == 0), stop=(kc == 3))
                    return last
                w = [("ACT", evPTE[idx]), evLDV[i]]
                prev = (i, j - 1, db) if j >= 1 else (i - 1, TJ - 1, db)
                w.append(("DVE", evOA.get(prev, ev_oacc_last[L - 1] if idx == 0 else 0)))
                S.add("tensor", pv, waits=w, inc="PE")
                evPV[(i, j, db)] = c("PE")
                woa = [("PE", evPV[(i, j, db)])]
                if idx == 0 and db == 0:
                    woa.append(("ST_K", ev_stores_kv[L]))
                S.add("vector",
                      lambda e, j=j, db=db: e.scalar_tensor_tensor(
                          oacc[:, j, db * 512:(db + 1) * 512],
                          oacc[:, j, db * 512:(db + 1) * 512],
                          alpha[:, j:j + 1],
                          psPV[:, db * 512:(db + 1) * 512], ALU.mult, ALU.add),
                      waits=woa, inc="DVE")
                evOA[(i, j, db)] = c("DVE")

        pe0_flash = c("PE")
        dve0_flash = c("DVE")
        load_k(0)
        load_v(0)
        for i in range(NSUB):
            pnew, pold = i % 2, 1 - (i % 2)

            def emit_s_sc(i, j, pnew, pold):
                idx = i * TJ + j
                bank = idx % 2
                def smm(e, j=j, bank=bank):
                    last = None
                    for dcin in range(DC):
                        last = e.matmul(psS[bank][:, :],
                                        qT[:, dcin, j * 128:(j + 1) * 128],
                                        kslot[:, dcin, :],
                                        start=(dcin == 0), stop=(dcin == DC - 1))
                    return last
                w = [("LD_K", evLD[i])]
                w.append(("ACT", evSC[idx - 2] if idx >= 2 else ev_qact))
                S.add("tensor", smm, waits=w, inc="PE")
                evS[(i, j)] = c("PE")

                wsc = [("PE", evS[(i, j)])]
                wsc.append(("DVE", evST[idx - 4] if idx >= 4 else dve0_flash))
                S.add("scalar",
                      lambda e, j=j, bank=bank: e.copy(sbf[:, j % 4, :], psS[bank][:, :]),
                      waits=wsc, inc="ACT")
                evSC[idx] = c("ACT")

                S.add("vector",
                      lambda e, j=j: e.tensor_reduce(nmraw[:, j:j + 1], sbf[:, j % 4, :],
                                                     AX.X, ALU.max, negate=True),
                      waits=[("ACT", evSC[idx])])
                S.add("vector", lambda e: e.drain())
                S.add("vector",
                      lambda e, j=j, pnew=pnew, pold=pold: e.tensor_scalar(
                          negm[:, pnew, j:j + 1], nmraw[:, j:j + 1],
                          SCALE, negm[:, pold, j:j + 1], ALU.mult, ALU.min))
                S.add("vector", lambda e: e.drain())
                S.add("vector",
                      lambda e, j=j, pnew=pnew, pold=pold: e.tensor_sub(
                          dcol[:, j:j + 1], negm[:, pnew, j:j + 1],
                          negm[:, pold, j:j + 1]),
                      inc="DVE")
                S.add("vector", lambda e: e.drain())
                evST[idx] = c("DVE")

            def emit_ap(i, j, pnew):
                idx = i * TJ + j
                wal = [("DVE", evST[idx])]
                wal.append(("DVE", evOA.get((i - 1, j, 3), ev_oacc_last[L - 1] if i == 0 else 0)))
                S.add("scalar",
                      lambda e, j=j: e.activation(alpha[:, j:j + 1], dcol[:, j:j + 1],
                                                  AF.Exp, bias=0.0, scale=1.0),
                      waits=wal)
                wp = []
                if idx >= 8:
                    wp.append(("PE", evT[idx - 8]))
                    wp.append(("DVE", evLU[idx - 8]))
                else:
                    wp.append(("PE", pe0_flash))
                    wp.append(("ST_V", ev_vlast_stg[L]))
                S.add("scalar",
                      lambda e, j=j, pnew=pnew: e.activation(
                          pex[:, j, :], sbf[:, j % 4, :], AF.Exp,
                          bias=negm[:, pnew, j:j + 1], scale=SCALE,
                          accum_out=lblk[:, j:j + 1]),
                      waits=wp, inc="ACT")
                evP[idx] = c("ACT")

                S.add("vector",
                      lambda e, j=j: e.scalar_tensor_tensor(
                          lrun[:, j:j + 1], lrun[:, j:j + 1], alpha[:, j:j + 1],
                          lblk[:, j:j + 1], ALU.mult, ALU.add),
                      waits=[("ACT", evP[idx])], inc="DVE")
                evLU[idx] = c("DVE")
                S.add("vector", lambda e: e.drain())

            emit_s_sc(i, 0, pnew, pold)
            for j in range(1, TJ):
                emit_s_sc(i, j, pnew, pold)
                emit_ap(i, j - 1, pnew)
            emit_ap(i, TJ - 1, pnew)
            if i + 1 < NSUB:
                load_k(i + 1)
            def emit_t(j):
                idx = i * TJ + j
                half = j % 2
                def tp(e, j=j, half=half):
                    last = None
                    for k in range(4):
                        last = e.transpose(psTs_k[half][:, k, :],
                                           pex[:, j, k * 128:(k + 1) * 128],
                                           sidb[:, :])
                    return last
                wt = [("ACT", evP[idx])]
                if idx >= 2:
                    wt.append(("ACT", evPTE[idx - 2]))
                S.add("tensor", tp, waits=wt, inc="PE")
                evT[idx] = c("PE")
                wpe = [("PE", evT[idx])]
                prev = (i, j - 2, 3) if j >= 2 else (i - 1, TJ - 1, 3)
                if prev in evPV:
                    wpe.append(("PE", evPV[prev]))
                S.add("scalar",
                      lambda e, j=j, half=half: e.copy(
                          ptsb[:, j % 2, :, :].rearrange("p a b -> p (a b)"),
                          psTs[half][:, :]),
                      waits=wpe, inc="ACT")
                evPTE[idx] = c("ACT")

            emit_t(0)
            emit_t(1)
            for j in range(1, TJ):
                emit_pv(i, j - 1)
                if j + 1 < TJ:
                    emit_t(j + 1)
            emit_pv(i, TJ - 1)
            if i + 1 < NSUB:
                load_v(i + 1)
        ev_oacc_last[L] = evOA[(NSUB - 1, TJ - 1, 3)]
        ev_pe_flash_last[L] = evPV[(NSUB - 1, TJ - 1, 3)]
        return c("GPS")

    def emit_tail(L, ev_gps_flash):
        S.add("vector", lambda e: e.reciprocal(rlv[:, :], lrun[:, :]))
        S.add("vector", lambda e: e.drain(), inc="DVE")
        pe0 = c("PE")
        act0 = c("ACT")
        evN = {}
        evTPlast = {-2: pe0, -1: pe0}
        evE = []
        gi = 0
        for j in range(TJ):
            S.add("vector",
                  lambda e, j=j: e.tensor_scalar(attnj[:, j % 2, :], oacc[:, j, :],
                                                 rlv[:, j:j + 1], None, ALU.mult),
                  waits=[("PE", evTPlast[j - 2])], inc="DVE")
            evN[j] = c("DVE")
            for g in range(4):
                half = gi % 2
                def tp(e, j=j, g=g, half=half):
                    last = None
                    for k in range(4):
                        dc = g * 4 + k
                        last = e.transpose(psTs_k[half][:, k, :],
                                           attnj[:, j % 2, dc * 128:(dc + 1) * 128],
                                           sidb[:, :])
                    return last
                w = [("DVE", evN[j])]
                w.append(("ACT", evE[gi - 2] if gi >= 2 else act0))
                S.add("tensor", tp, waits=w, inc="PE")
                evTPlast[j] = c("PE")
                S.add("scalar",
                      lambda e, j=j, g=g, half=half: e.copy(
                          attnT[:, g * 4:(g + 1) * 4, j * 128:(j + 1) * 128],
                          psTs_k[half][:, :, :]),
                      waits=[("PE", evTPlast[j])], inc="ACT")
                evE.append(c("ACT"))
                gi += 1
        ev_norm_all = evN[TJ - 1]
        for s4 in range(4):
            prev_proj_mm[s4] = max(prev_proj_mm[s4], ev_pe_flash_last[L])
        ev_omm, ev_oact = emit_projT(L, woc, sbo, xT, attnT,
                                     extra_first=[("DVE", ev_norm_all)])
        tail_omm[0] = ev_omm
        if L == NL - 1:
            evEv = []
            gi = 0
            for dc in range(DC):
                for g in range(2):
                    half = gi % 2
                    def tp(e, dc=dc, g=g, half=half):
                        last = None
                        for k in range(4):
                            t = g * 4 + k
                            last = e.transpose(psTs_k[half][:, k, :],
                                               xT[:, dc, t * 128:(t + 1) * 128],
                                               sidb[:, :])
                        return last
                    w = [("ACT", ev_oact)]
                    if gi >= 2:
                        w.append(("DVE", evEv[gi - 2]))
                    S.add("tensor", tp, waits=w, inc="PE")
                    evt = c("PE")
                    S.add("vector",
                          lambda e, dc=dc, g=g, half=half: e.tensor_add(
                              x0rm[:, g * 4:(g + 1) * 4, dc * 128:(dc + 1) * 128],
                              x0rm[:, g * 4:(g + 1) * 4, dc * 128:(dc + 1) * 128],
                              psTs_k[half][:, :, :]),
                          waits=[("PE", evt)], inc="DVE")
                    evEv.append(c("DVE"))
                    gi += 1
            S.add("vector", lambda e: e.drain(), inc="DVE")
        return c("DVE"), ev_norm_all

    ev_resid = 0
    ev_norm_prev = [0]
    tail_omm = [0]
    for L in range(NL):
        if L == 0:
            ev_ln1 = emit_ln(lambda t: x0rm[:, t, :],
                             {0: [("LD_X0", 16)], 4: [("LD_X1", 16)]},
                             {0: [("LD_X0", 16), ("LD_INIT", ev_x0)],
                              4: [("LD_X1", 16)]})
            emit_rm_to_ct(ln_row, xT, ev_ln1,
                          extra0=[("LD_INIT", ev_x0)])

        S.add("gpsimd", lambda e: e.memset(negm[:, 1, :], NEG_BIG),
              waits=[("ACT", c("ACT"))])
        S.add("gpsimd", lambda e: e.drain(), inc="GPS")

        if L == 0:
            emit_wprep()
        for s4 in range(4):
            prev_proj_mm[s4] = max(prev_proj_mm[s4], ev_pe_flash_last[L - 1])
        kx = ([("DVE", ev_norm_prev[0])] if L > 0 else
              [("DVE", ev_wprep_dve[0]), ("ACT", ev_wprep_act[0])])
        ev_kmm, ev_kact = emit_projT(L, wkc, sbk, kTst, xT, extra_first=kx)
        buf = L % 2
        evst_k = []
        for s2 in range(2):
            dma("gpsimd", "ST_K", kv_in_v[s2].rearrange("p (a b) -> p a b", b=512),
                kTst[:, :, s2 * 512:(s2 + 1) * 512],
                waits=[("ACT", ev_kact)])
            evst_k.append(c("ST_K"))

        def issue_kcc(s2):
            S.add("gpsimd",
                  lambda e, s2=s2, buf=buf: e.collective_compute(
                      "AllGather", ALU.bypass, replica_groups=[list(range(NCORE))],
                      ins=[kv_in[s2 * 128:(s2 + 1) * 128, :].opt()],
                      outs=[g_k[s2][buf].opt()]),
                  waits=[("ST_K", evst_k[s2])], inc="CCK%d" % s2)
            ev_cc[(L, "k", s2)] = ("CCK%d" % s2, c("CCK%d" % s2))
        issue_kcc(0)

        dma("sync", "LD_WV", bvb[:, :], bvn[L:L + 1, :],
            waits=[("PE", ev_pe_flash_last[L - 1])])
        wv_v = wvn[L].rearrange("(a p) d -> p a d", p=128)
        w_wv = [("PE", ev_pe_flash_last[L - 1])]
        dma("sync", "LD_WV", wvlo8[:, :, :], wv_v[:, 0:8, :], waits=w_wv)
        w_wvh = ([("PE", ev_wprep_pe[0]), ("ST_W", ev_wprep_stg[0])] if L == 0
                 else [("PE", tail_omm[0])])
        dma("sync", "LD_WV", wvhi8[:, :, :], wv_v[:, 8:16, :], waits=w_wvh)
        ev_wv = c("LD_WV")
        ev_vact = []
        st_v_hist = []
        for m in range(TJ):
            def vmm(e, m=m):
                last = None
                for dcin in range(DC):
                    src = wvlo8 if dcin < 8 else wvhi8
                    for db in range(4):
                        last = e.matmul(psPV[:, db * 512:(db + 1) * 512],
                                        xT[:, dcin, m * 128:(m + 1) * 128],
                                        src[:, dcin % 8, db * 512:(db + 1) * 512],
                                        start=(dcin == 0), stop=False)
                for db in range(4):
                    last = e.matmul(psPV[:, db * 512:(db + 1) * 512],
                                    sonb[:, :], bvb[0:1, db * 512:(db + 1) * 512],
                                    start=False, stop=True)
                return last
            w = [("LD_WV", ev_wv)]
            if m == 0:
                w.append(("DVE", ev_oacc_last[L - 1]))
                w.append(("ACT", c("ACT")))
            else:
                w.append(("ACT", ev_vact[-1]))
            S.add("tensor", vmm, waits=w, inc="PE")
            ev_vmm = c("PE")
            base = (m % 2) * 4
            for db in range(4):
                wcp = [("PE", ev_vmm)]
                if db == 0 and m >= 2:
                    wcp.append(("ST_V", st_v_hist[m - 2]))
                S.add("scalar",
                      lambda e, db=db, base=base: e.copy(
                          pex[:, base + db, :], psPV[:, db * 512:(db + 1) * 512]),
                      waits=wcp, inc="ACT")
            eva = c("ACT")
            dma("gpsimd", "ST_V",
                kv_in_v[2 + m // 4][:, (m % 4) * 2048:(m % 4 + 1) * 2048],
                pex[:, base:base + 4, :], waits=[("ACT", eva)])
            ev_vact.append(c("ACT"))
            st_v_hist.append(c("ST_V"))
            if m == 3 or m == 7:
                s2 = m // 4
                evst = c("ST_V")
                S.add("gpsimd",
                      lambda e, s2=s2, buf=buf: e.collective_compute(
                          "AllGather", ALU.bypass, replica_groups=[list(range(NCORE))],
                          ins=[kv_in[(2 + s2) * 128:(3 + s2) * 128, :].opt()],
                          outs=[g_v[s2][buf].opt()]),
                      waits=[("ST_V", evst)], inc="CCV%d" % s2)
                ev_cc[(L, "v", s2)] = ("CCV%d" % s2, c("CCV%d" % s2))
                if m == 3:
                    issue_kcc(1)
        ev_vlast[L] = c("PE")
        ev_vlast_stg[L] = c("ST_V")
        ev_stores_kv[L] = c("ST_K")
        for s3 in ({0: [0], 1: [1], 2: [2], 3: [3]}.get(L, [])):
            S.add("gpsimd",
                  lambda e, s3=s3: e.collective_compute(
                      "AllGather", ALU.bypass, replica_groups=[list(range(NCORE))],
                      ins=[w_in[s3 * 1024:(s3 + 1) * 1024, :].opt()],
                      outs=[w_alls[s3].opt()]),
                  waits=[("ST_W", ev_wprep_stg[0])], inc="CCW")
        ev_cc["Wall"] = ("CCW", 4)

        ev_qmm, ev_qact = emit_projT(L, wqc, sbq, qT, xT)
        ev_qmm_flash[0] = ev_qmm
        S.add("vector", lambda e: e.memset(oacc[:, :, :], 0.0),
              waits=[("PE", ev_qmm), ("ST_K", c("ST_K"))])
        S.add("vector", lambda e: e.drain())
        ev_gps_flash = emit_flash(L, ev_qact)
        ev_resid, ev_na = emit_tail(L, ev_gps_flash)
        ev_norm_prev[0] = ev_na

    ev_ln2 = emit_ln(lambda t: x0rm[:, t, :], [("DVE", ev_resid)],
                     [("PE", c("PE")), ("DVE", ev_resid)])
    emit_rm_to_ct(lambda t: x0rm[:, t, :], xT, ev_resid)
    act0 = c("ACT")
    evX = []
    gi = 0
    for dc in range(DC):
        for g in range(2):
            half = gi % 2
            def tp(e, dc=dc, g=g, half=half):
                last = None
                for k in range(4):
                    t = g * 4 + k
                    last = e.transpose(psTs_k[half][:, k, :],
                                       ln_row(t)[:, dc * 128:(dc + 1) * 128], sidb[:, :])
                return last
            w = [("DVE", ev_ln2), ("ACT", evX[gi - 2] if gi >= 2 else act0)]
            S.add("tensor", tp, waits=w, inc="PE")
            evt = c("PE")
            S.add("scalar",
                  lambda e, dc=dc, g=g, half=half: e.copy(
                      XT[:, dc, g * 512:(g + 1) * 512], psTs[half][:, :]),
                  waits=[("PE", evt)], inc="ACT")
            evX.append(c("ACT"))
            gi += 1
    ev_xtf = c("ACT")

    dve_ffn0 = c("DVE")
    pe_ffn0 = c("PE")
    prev_act_gate = ev_xtf
    evd_hist = []
    evB_all = []
    ev_stg_act_prev = None
    st_out_hist = []
    for half in range(2):
        evA = {}
        evGel = {}
        for ncx in range(NC2):
            r, lc = ncx // 8, ncx % 8
            slot = ncx % 2
            fsem = "LD_F0%d" % slot
            wld = [ev_cc["Wall"]]
            wld.append(("PE", evA[ncx - 2] if ncx >= 2 else c("PE")))
            for hl in range(2):
                dma("sync", fsem, w0buf[:, slot, hl, :], w_all_v[hl][r, lc, :, :],
                    waits=wld if hl == 0 else [])
            evw = (fsem, c(fsem))
            bank = ncx % 2
            def amm(e, slot=slot, bank=bank, half=half):
                last = None
                w0v = w0buf.ap().rearrange("p s l (a b) -> p s l a b", a=DC)
                for dc in range(DC):
                    for hl in range(2):
                        last = e.matmul(psS[bank][:, :],
                                        w0v[:, slot, hl, dc, :],
                                        XT[:, dc, half * 512:(half + 1) * 512],
                                        start=(dc == 0 and hl == 0),
                                        stop=(dc == DC - 1 and hl == 1))
                return last
            w = [evw, ("ACT", prev_act_gate if ncx < 2 else evGel[ncx - 2])]
            S.add("tensor", amm, waits=w, inc="PE")
            evA[ncx] = c("PE")
            dtgt = d16a if ncx < 32 else d16b
            S.add("scalar",
                  lambda e, ncx=ncx, bank=bank, dtgt=dtgt: e.activation(
                      dtgt[:, ncx % 32, :], psS[bank][:, :], AF.Gelu,
                      bias=sb0[:, ncx:ncx + 1], scale=1.0),
                  waits=[("PE", evA[ncx])], inc="ACT")
            evGel[ncx] = c("ACT")
        ev_gelu = c("ACT")
        for pas in range(4):
            for ncx in range(NC2):
                r, lc = ncx // 8, ncx % 8
                g = len(evB_all)
                slot = g % 8
                fsem = "LD_F1%d" % slot
                wld = [("PE", evB_all[g - 8] if g >= 8 else pe_ffn0),
                       ev_cc["Wall"]]
                for hl in range(2):
                    dma("sync", fsem, w1buf[:, slot, hl, :],
                        w_all_v[2 + hl][r, lc, :, pas * 512:(pas + 1) * 512],
                        waits=wld if hl == 0 else [])
                evw = (fsem, c(fsem))
                dsrc = d16a if ncx < 32 else d16b
                def bmm(e, ncx=ncx, slot=slot, dsrc=dsrc):
                    last = None
                    w1v = w1buf.ap().rearrange("p s l (a b) -> p s l a b", a=4)
                    for hl in range(2):
                        for hc in range(4):
                            last = e.matmul(psPV[:, hc * 512:(hc + 1) * 512],
                                            w1v[:, slot, hl, hc, :],
                                            dsrc[:, ncx % 32, :],
                                            start=(ncx == 0 and hl == 0),
                                            stop=(ncx == NC2 - 1 and hl == 1))
                    return last
                w = [evw, ("ACT", ev_gelu)]
                if ncx == 0:
                    if ev_stg_act_prev is not None:
                        w.append(("ACT", ev_stg_act_prev))
                    else:
                        w.append(("DVE", dve_ffn0))
                        w.append(("ACT", c("ACT")))
                S.add("tensor", bmm, waits=w, inc="PE")
                evB_all.append(c("PE"))
            evB = evB_all[-1]
            for hc in range(4):
                hch = pas * 4 + hc
                we = [("PE", evB)]
                if len(evd_hist) >= 4:
                    we.append(("DVE", evd_hist[-4]))
                S.add("scalar",
                      lambda e, hc=hc, hch=hch: e.activation(
                          e16s[:, hc, :], psPV[:, hc * 512:(hc + 1) * 512],
                          AF.Identity, bias=sb1[:, hch:hch + 1], scale=1.0),
                      waits=we, inc="ACT")
                eva = c("ACT")
                wadd = [("ACT", eva)]
                if len(st_out_hist) >= 4:
                    wadd.append(("ST_OUT", st_out_hist[-4]))
                S.add("vector",
                      lambda e, hc=hc, hch=hch, half=half: e.tensor_add(
                          outb[:, hc, :],
                          xT[:, hch, half * 512:(half + 1) * 512],
                          e16s[:, hc, :]),
                      waits=wadd, inc="DVE")
                evd = c("DVE")
                evd_hist.append(evd)
                dma("gpsimd", "ST_OUT",
                    y_t.rearrange("(a p) r -> a p r", p=128)[pas * 4 + hc]
                       [:, half * 512:(half + 1) * 512],
                    outb[:, hc, :], waits=[("DVE", evd)])
                st_out_hist.append(c("ST_OUT"))
            ev_stg_act_prev = c("ACT")
        prev_act_gate = c("ACT")
    ev_out = c("ST_OUT")
    S.add("gpsimd", None, waits=[("ST_OUT", ev_out)])

    from contextlib import ExitStack
    sems = {}
    es = ExitStack()
    with es:
        for sname in ("LD_INIT", "LD_X0", "LD_X1",
                      "LD_PW0", "LD_PW1", "LD_PW2", "LD_PW3",
                      "LD_WV", "LD_WCH0", "LD_WCH1",
                      "LD_K", "LD_V0", "LD_V1", "LD_F00", "LD_F01",
                      "LD_F10", "LD_F11", "LD_F12", "LD_F13",
                      "LD_F14", "LD_F15", "LD_F16", "LD_F17",
                      "ST_K", "ST_V", "ST_W", "ST_OUT",
                      "CCK0", "CCK1", "CCV0", "CCV1", "CCW",
                      "PE", "ACT", "DVE", "GPS"):
            sems[sname] = es.enter_context(nc.semaphore(sname))
        blk = es.enter_context(nc.Block())
        S.emit(blk, sems)
    n_items = {e: len(S.items[e]) for e in ENGS}
    return nc, n_items


_NC_CACHE = None


def kernel(**inputs):
    global _NC_CACHE
    xs = np.asarray(inputs["L0_attn_x"])
    Wq = np.asarray(inputs["L0_attn_Wq"])
    bq = np.asarray(inputs["L0_attn_bq"])
    Wk = np.asarray(inputs["L0_attn_Wk"])
    bk = np.asarray(inputs["L0_attn_bk"])
    Wv = np.asarray(inputs["L0_attn_Wv"])
    bv = np.asarray(inputs["L0_attn_bv"])
    Wo = np.asarray(inputs["L0_attn_Wo"])
    bo = np.asarray(inputs["L0_attn_bo"])
    W0 = np.asarray(inputs["L0_ffn_W0"], dtype=np.float32)
    bias0 = np.asarray(inputs["L0_ffn_bias0"], dtype=np.float32)
    W1 = np.asarray(inputs["L0_ffn_W1"], dtype=np.float32)
    bias1 = np.asarray(inputs["L0_ffn_bias1"], dtype=np.float32)

    x2 = np.ascontiguousarray(xs.reshape(SEQ, D))

    def colchunk(W):
        return np.ascontiguousarray(
            W.reshape(NL, DC, 128, DC, 128).transpose(0, 3, 2, 1, 4))

    def biasrc(b):
        return np.ascontiguousarray(b.reshape(NL, DC, 128).transpose(2, 0, 1))

    common = {
        "wqc": colchunk(Wq), "wkc": colchunk(Wk), "woc": colchunk(Wo),
        "wvn": np.ascontiguousarray(Wv),
        "bqr": biasrc(bq), "bkr": biasrc(bk), "bor": biasrc(bo),
        "bvn": np.ascontiguousarray(bv),
        "b0r": np.ascontiguousarray(bias0.reshape(NC2, 128).T),
        "b1r": np.ascontiguousarray(bias1.reshape(DC, 128).T),
        "identb": np.eye(128, dtype=ml_dtypes.bfloat16),
        "identf": np.eye(128, dtype=np.float32),
        "onesb": np.ones((1, 128), dtype=ml_dtypes.bfloat16),
    }
    in_maps = []
    for cid in range(NCORE):
        m = dict(common)
        m["x"] = np.ascontiguousarray(x2[cid * RPC:(cid + 1) * RPC])
        m["w0s"] = np.ascontiguousarray(W0[cid * RPC:(cid + 1) * RPC])
        m["w1s"] = np.ascontiguousarray(W1[:, cid * RPC:(cid + 1) * RPC])
        in_maps.append(m)

    if _NC_CACHE is None:
        _NC_CACHE = build_nc()[0]
    res = run_bass_kernel_spmd(_NC_CACHE, in_maps, core_ids=list(range(NCORE)))
    parts = [np.asarray(res.results[cid]["y_t"]).T for cid in range(NCORE)]
    out = np.concatenate(parts, axis=0).reshape(1, SEQ, D)
    return out


if __name__ == "__main__":
    nc, n = build_nc()
    print("built ok; items:", n, "total:", sum(n.values()))


# version 21 (final)
# speedup vs baseline: 1.0870x; 1.0026x over previous
import math
import numpy as np
import ml_dtypes
import concourse.bass as bass
import concourse.mybir as mybir
from concourse.bass_utils import run_bass_kernel_spmd

F32 = mybir.dt.float32
BF16 = mybir.dt.bfloat16
F16 = mybir.dt.float16
AX = mybir.AxisListType
ALU = mybir.AluOpType
AF = mybir.ActivationFunctionType

NCORE = 8
SEQ = 8192
D = 2048
RPC = SEQ // NCORE
NL = 4
NFF = 8192
DC = D // 128
TJ = RPC // 128
NSUB = 16
NC2 = NFF // 128
SCALE = 1.0 / math.sqrt(D)
NEG_BIG = 3.0e38

ENGS = ("sync", "tensor", "vector", "scalar", "gpsimd")


class Sched:

    def __init__(self):
        self.items = {e: [] for e in ENGS}
        self.cnt = {}

    def c(self, sem):
        return self.cnt.get(sem, 0)

    def add(self, eng, fn, waits=(), inc=None, inck=1):
        val = None
        if inc is not None:
            self.cnt[inc] = self.cnt.get(inc, 0) + inck
            val = self.cnt[inc]
        ws = [(s, v) for (s, v) in waits if v and v > 0]
        self.items[eng].append((fn, ws, inc, inck))
        return val

    def emit(self, blk, sems):
        def mk(name):
            def run(eng):
                seen = {}
                for fn, waits, inc, inck in self.items[name]:
                    for (s, v) in waits:
                        if seen.get(s, 0) < v:
                            eng.wait_ge(sems[s], v)
                            seen[s] = v
                    inst = fn(eng) if fn is not None else None
                    if inc is not None:
                        if inst is not None:
                            inst.then_inc(sems[inc], inck)
                        else:
                            eng.sem_inc(sems[inc], inck)
            return run
        blk.sync(mk("sync"))
        blk.tensor(mk("tensor"))
        blk.vector(mk("vector"))
        blk.scalar(mk("scalar"))
        blk.gpsimd(mk("gpsimd"))


def build_nc():
    nc = bass.Bass("TRN2", target_bir_lowering=False)

    def ein(name, shape, dt):
        return nc.dram_tensor(name, shape, dt, kind="ExternalInput").ap()

    x_in = ein("x", [RPC, D], BF16)
    wqc = ein("wqc", [NL, DC, 128, DC, 128], BF16)
    wkc = ein("wkc", [NL, DC, 128, DC, 128], BF16)
    woc = ein("woc", [NL, DC, 128, DC, 128], BF16)
    wvn = ein("wvn", [NL, D, D], BF16)
    bqr = ein("bqr", [128, NL, DC], BF16)
    bkr = ein("bkr", [128, NL, DC], BF16)
    bor = ein("bor", [128, NL, DC], BF16)
    bvn = ein("bvn", [NL, D], BF16)
    w0s = ein("w0s", [RPC, D], F32)
    w1s = ein("w1s", [D, RPC], F32)
    b0r = ein("b0r", [128, NC2], F32)
    b1r = ein("b1r", [128, DC], F32)
    identb = ein("identb", [128, 128], BF16)
    identf = ein("identf", [128, 128], F32)
    onesb = ein("onesb", [1, 128], BF16)

    y_t = nc.dram_tensor("y_t", [D, RPC], F32, kind="ExternalOutput").ap()

    kv_in = nc.dram_tensor("kv_in", [4 * 128, SEQ], BF16).ap()
    g_k = [[nc.dram_tensor(f"g_k{s}{b}", [NCORE * 128, SEQ], BF16,
                           addr_space="Shared").ap()
            for b in range(2)] for s in range(2)]
    g_v = [[nc.dram_tensor(f"g_v{s}{b}", [NCORE * 128, SEQ], BF16,
                           addr_space="Shared").ap()
            for b in range(2)] for s in range(2)]
    w_in = nc.dram_tensor("w_in", [4 * 8 * 128, D], F16).ap()
    w_alls = [nc.dram_tensor(f"w_all{s}", [NCORE * 8 * 128, D], F16,
                             addr_space="Shared").ap()
              for s in range(4)]

    kv_in_v = kv_in.rearrange("(a p) n -> a p n", p=128)
    g_k_v = [[g_k[s][b].rearrange("(r p) n -> r p n", p=128) for b in range(2)] for s in range(2)]
    g_v_v = [[g_v[s][b].rearrange("(r p) n -> r p n", p=128) for b in range(2)] for s in range(2)]
    w_in_v = w_in.rearrange("(s l p) h -> s l p h", l=8, p=128)
    w_in_pv = w_in.rearrange("(s l p) h -> s p l h", l=8, p=128)
    w_all_v = [w_alls[s].rearrange("(r l p) h -> r l p h", l=8, p=128)
               for s in range(4)]

    def sb(name, shape, dt, off):
        return nc.alloc_sbuf_tensor_at(name, shape, dt, offset=off)

    BASE = 16512
    x0rm = sb("x0rm", [128, TJ, D], BF16, BASE)
    XT = sb("XT", [128, DC, RPC], F16, BASE)
    RA = BASE + 32768
    lnout = sb("lnout", [128, TJ, D], BF16, RA)
    kTst = sb("kTst", [128, DC, RPC], BF16, RA)
    d16b = sb("d16b", [128, 32, 512], F16, RA)
    oacc = sb("oacc", [128, TJ, D], F32, RA)
    xT = sb("xT", [128, DC, RPC], BF16, RA + 32768)
    RB = RA + 65536
    qT = sb("qT", [128, DC, RPC], BF16, RB)
    attnT = sb("attnT", [128, DC, RPC], BF16, RB)
    wch = sb("wch", [128, D], F32, RB)
    w0stg = sb("w0stg", [128, 2, DC, 128], F16, RB + 8192)
    w1stg = sb("w1stg", [128, 2, 8, 4, 128], F16, RB + 16384)
    wvhi8 = sb("wvhi8", [128, 8, D], BF16, RB)
    w0buf = sb("w0buf", [128, 2, 2, DC * 128], F16, RB)
    w1buf = sb("w1buf", [128, 8, 2, 512], F16, RB + 16384)
    KV = RB + 32768
    kslot = sb("kslot", [128, DC, 512], BF16, KV)
    vslot = sb("vslot", [128, 4, D], BF16, KV + 16384)
    wvlo8 = sb("wvlo8", [128, 8, D], BF16, KV)
    d16a = sb("d16a", [128, 32, 512], F16, KV)
    SM = KV + 32768
    projw = sb("projw", [128, 4, DC, 128], BF16, SM)
    pex = sb("pex", [128, TJ, 512], BF16, SM + 16384)
    e16s = sb("e16s", [128, 4, 512], F16, SM + 16384)
    ptsb = sb("ptsb", [128, 2, 4, 128], BF16, SM + 24576)
    attnj = sb("attnj", [128, 2, D], BF16, SM + 26624)
    vslot2 = sb("vslot2", [128, 4, D], BF16, SM)
    wch2 = sb("wch2", [128, D], F32, SM + 26624)
    outb = sb("outb", [128, 4, 512], F32, SM + 26624)
    bvb = sb("bvb", [1, D], BF16, SM + 34816)
    sbq = sb("sbq", [128, NL, DC], BF16, SM + 38912)
    sbk = sb("sbk", [128, NL, DC], BF16, SM + 39040)
    sbo = sb("sbo", [128, NL, DC], BF16, SM + 39168)
    sb0 = sb("sb0", [128, NC2], F32, SM + 39296)
    sb1 = sb("sb1", [128, DC], F32, SM + 39552)
    sidb = sb("sidb", [128, 128], BF16, SM + 39616)
    sidf = sb("sidf", [128, 128], F32, SM + 39872)
    sonb = sb("sonb", [1, 128], BF16, SM + 40384)
    ST = SM + 40640
    negm = sb("negm", [128, 2, TJ], F32, ST)
    nmraw = sb("nmraw", [128, TJ], F32, ST + 64)
    dcol = sb("dcol", [128, TJ], F32, ST + 96)
    alpha = sb("alpha", [128, TJ], F32, ST + 128)
    lblk = sb("lblk", [128, TJ], F32, ST + 160)
    lrun = sb("lrun", [128, TJ], F32, ST + 192)
    rlv = sb("rlv", [128, TJ], F32, ST + 224)
    lnsum = sb("lnsum", [128, TJ], F32, ST + 256)
    lnsq = sb("lnsq", [128, TJ], F32, ST + 288)
    lnm = sb("lnm", [128, TJ], F32, ST + 320)
    lnr = sb("lnr", [128, TJ], F32, ST + 352)
    sbf = sb("sbf", [128, 4, 512], BF16, ST + 512)
    assert ST + 512 + 4096 <= 229376

    psS = [nc.alloc_psum_tensor("psS0", [128, 512], F32),
           nc.alloc_psum_tensor("psS1", [128, 512], F32)]
    psTs = [nc.alloc_psum_tensor("psT0", [128, 512], BF16),
            nc.alloc_psum_tensor("psT1", [128, 512], BF16)]
    psPV = nc.alloc_psum_tensor("psPV", [128, 2048], F32)
    psTs_k = [t.ap().rearrange("p (k d) -> p k d", k=4) for t in psTs]

    S = Sched()
    c = S.c

    def dma(eng, sem, out, in_, waits=()):
        return S.add(eng, lambda e, o=out, i=in_: e.dma_start(out=o, in_=i),
                     waits=waits, inc=sem, inck=16)

    x_in_v = x_in.rearrange("(t p) d -> p t d", p=128)
    for h in range(2):
        dma("sync", "LD_X%d" % h, x0rm[:, 4 * h:4 * h + 4, :],
            x_in_v[:, 4 * h:4 * h + 4, :])
    dma("sync", "LD_INIT", sbq[:, :, :], bqr)
    dma("sync", "LD_INIT", sbk[:, :, :], bkr)
    dma("sync", "LD_INIT", sbo[:, :, :], bor)
    dma("sync", "LD_INIT", sb0[:, :], b0r)
    dma("sync", "LD_INIT", sb1[:, :], b1r)
    dma("sync", "LD_INIT", sidb[:, :], identb)
    dma("sync", "LD_INIT", sidf[:, :], identf)
    dma("sync", "LD_INIT", sonb[:, :], onesb)
    ev_x0 = c("LD_INIT")

    S.add("gpsimd", lambda e: e.memset(negm[:, :, :], NEG_BIG))
    S.add("gpsimd", lambda e: e.memset(lrun[:, :], 0.0))
    S.add("gpsimd", lambda e: e.drain(), inc="GPS")

    def ln_row(t):
        return lnout[:, t, :]

    def emit_ln(src_row, wait_dve, wait_act):
        wd = wait_dve if isinstance(wait_dve, dict) else {0: wait_dve}
        wa = wait_act if isinstance(wait_act, dict) else {0: wait_act}
        for t in range(TJ):
            S.add("vector",
                  lambda e, t=t: e.reduce_sum(lnsum[:, t:t + 1], src_row(t), AX.X),
                  waits=wd.get(t, []), inc="DVE")
        ev_sums = c("DVE")
        for t in range(TJ):
            S.add("scalar",
                  lambda e, t=t: e.activation(ln_row(t), src_row(t), AF.Square,
                                              bias=0.0, scale=1.0,
                                              accum_out=lnsq[:, t:t + 1]),
                  waits=wa.get(t, []), inc="ACT")
        ev_sq = c("ACT")
        S.add("vector", lambda e: e.tensor_scalar(lnm[:, :], lnsum[:, :], 1.0 / D, None, ALU.mult),
              waits=[("DVE", ev_sums)])
        S.add("vector", lambda e: e.drain())
        S.add("vector", lambda e: e.tensor_scalar(lnr[:, :], lnsq[:, :], 1.0 / D, None, ALU.mult),
              waits=[("ACT", ev_sq)])
        S.add("vector", lambda e: e.drain())
        S.add("vector", lambda e: e.tensor_mul(lnsq[:, :], lnm[:, :], lnm[:, :]))
        S.add("vector", lambda e: e.drain())
        S.add("vector", lambda e: e.tensor_sub(lnr[:, :], lnr[:, :], lnsq[:, :]))
        S.add("vector", lambda e: e.drain())
        S.add("vector", lambda e: e.tensor_scalar(lnr[:, :], lnr[:, :], 1e-5, None, ALU.add))
        S.add("vector", lambda e: e.drain(), inc="DVE")
        ev_var = c("DVE")
        S.add("scalar", lambda e: e.activation(lnr[:, :], lnr[:, :], AF.Ln, bias=0.0, scale=1.0),
              waits=[("DVE", ev_var)])
        S.add("scalar", lambda e: e.drain())
        S.add("scalar", lambda e: e.activation(lnr[:, :], lnr[:, :], AF.Exp, bias=0.0, scale=-0.5))
        S.add("scalar", lambda e: e.drain(), inc="ACT")
        ev_rstd = c("ACT")
        for t in range(TJ):
            S.add("vector",
                  lambda e, t=t: e.tensor_scalar(ln_row(t), src_row(t),
                                                 lnm[:, t:t + 1], lnr[:, t:t + 1],
                                                 ALU.subtract, ALU.mult),
                  waits=[("ACT", ev_rstd)] if t == 0 else [], inc="DVE")
        return c("DVE")

    def emit_rm_to_ct(src_row, dst_ct, wait_dve_val, extra0=()):
        act0 = c("ACT")
        evs = []
        gidx = 0
        for dc in range(DC):
            for g in range(2):
                half = gidx % 2
                def tp(e, dc=dc, g=g, half=half):
                    last = None
                    for k in range(4):
                        t = g * 4 + k
                        last = e.transpose(psTs_k[half][:, k, :],
                                           src_row(t)[:, dc * 128:(dc + 1) * 128],
                                           sidb[:, :])
                    return last
                w = []
                if gidx < 2:
                    w = [("DVE", wait_dve_val), ("ACT", act0)] + list(extra0)
                else:
                    w = [("ACT", evs[gidx - 2])]
                S.add("tensor", tp, waits=w, inc="PE")
                evt = c("PE")
                S.add("scalar",
                      lambda e, dc=dc, g=g, half=half: e.copy(
                          dst_ct[:, dc, g * 512:(g + 1) * 512], psTs[half][:, :]),
                      waits=[("PE", evt)], inc="ACT")
                evs.append(c("ACT"))
                gidx += 1
        return c("PE"), c("ACT")

    prev_proj_mm = [0, 0, 0, 0]

    def emit_projT(L, wsrc, bias_sb, dst, rhs_src, extra_first=()):
        act0 = c("ACT")
        ev_mm = []
        ev_ev = []
        for doc in range(DC):
            slot = doc % 4
            psem = "LD_PW%d" % slot
            wld = [("PE", ev_mm[2 * doc - 7] if doc >= 4 else prev_proj_mm[slot])]
            dma("sync", psem, projw[:, slot, :, :].rearrange("p a b -> p (a b)"),
                wsrc[L, doc].rearrange("p a b -> p (a b)"), waits=wld)
            evw = (psem, c(psem))
            for half in range(2):
                n = 2 * doc + half
                bank = n % 2
                def mm(e, doc=doc, half=half, slot=slot, bank=bank):
                    last = None
                    for dcin in range(DC):
                        last = e.matmul(psS[bank][:, :],
                                        projw[:, slot, dcin, :],
                                        rhs_src[:, dcin, half * 512:(half + 1) * 512],
                                        start=(dcin == 0), stop=(dcin == DC - 1))
                    return last
                w = [evw]
                w.append(("ACT", ev_ev[n - 2] if n >= 2 else act0))
                if n < 2:
                    w.extend(extra_first)
                S.add("tensor", mm, waits=w, inc="PE")
                ev_mm.append(c("PE"))
                S.add("scalar",
                      lambda e, doc=doc, half=half, bank=bank: e.activation(
                          dst[:, doc, half * 512:(half + 1) * 512], psS[bank][:, :],
                          AF.Identity, bias=bias_sb[:, L, doc:doc + 1], scale=1.0),
                      waits=[("PE", ev_mm[-1])], inc="ACT")
                ev_ev.append(c("ACT"))
        prev_proj_mm[0] = ev_mm[-7]
        prev_proj_mm[1] = ev_mm[-5]
        prev_proj_mm[2] = ev_mm[-3]
        prev_proj_mm[3] = ev_mm[-1]
        return ev_mm[-1], ev_ev[-1]

    ev_wprep_pe = [0]
    ev_wprep_dve = [0]
    ev_wprep_act = [0]
    ev_wprep_stg = [0]

    def emit_wprep():
        act0 = c("ACT")
        wchs = [wch, wch2]
        bank_gate = {0: (0, 0), 1: (0, 0)}
        evtp_rt = {-2: 0, -1: 0}
        for rt in range(8):
            csem = "LD_WCH%d" % (rt % 2)
            dma("sync", csem, wchs[rt % 2][:, :],
                w0s.rearrange("(a p) d -> a p d", p=128)[rt],
                waits=[("PE", evtp_rt[rt - 2])])
            evl = (csem, c(csem))
            for g in range(4):
                bank = g % 2
                def tp(e, rt=rt, g=g, bank=bank):
                    psv = psS[bank].ap().rearrange("p (k d) -> p k d", k=4)
                    last = None
                    for k in range(4):
                        dc = g * 4 + k
                        last = e.transpose(psv[:, k, :],
                                           wchs[rt % 2][:, dc * 128:(dc + 1) * 128],
                                           sidf[:, :])
                    return last
                S.add("tensor", tp, waits=[evl, ("DVE", bank_gate[bank][1]),
                                           ("ACT", bank_gate[bank][0] or act0)], inc="PE")
                evtp_rt[rt] = c("PE")
                evt = c("PE")
                S.add("scalar",
                      lambda e, g=g, bank=bank: e.copy(
                          w0stg[:, 0, g * 4:(g + 1) * 4, :].rearrange("p a b -> p (a b)"),
                          psS[bank][:, :]),
                      waits=[("PE", evt), ("ST_W", c("ST_W"))], inc="ACT")
                evc = c("ACT")
                S.add("vector",
                      lambda e, g=g, bank=bank: e.tensor_sub(
                          w0stg[:, 1, g * 4:(g + 1) * 4, :].rearrange("p a b -> p (a b)"),
                          psS[bank][:, :],
                          w0stg[:, 0, g * 4:(g + 1) * 4, :].rearrange("p a b -> p (a b)")),
                      waits=[("ACT", evc)], inc="DVE")
                bank_gate[bank] = (evc, c("DVE"))
            evd = c("DVE")
            for hl in range(2):
                dma("gpsimd", "ST_W", w_in_v[hl, rt, :, :],
                    w0stg[:, hl, :, :].rearrange("p a b -> p (a b)"),
                    waits=[("DVE", evd)])
        evtp_ht = {-2: evtp_rt[6], -1: evtp_rt[7]}
        for htg in range(4):
            for hh in range(4):
                ht = htg * 4 + hh
                csem = "LD_WCH%d" % (ht % 2)
                dma("sync", csem, wchs[ht % 2][:, 0:1024],
                    w1s.rearrange("(a p) n -> a p n", p=128)[ht],
                    waits=[("PE", evtp_ht[ht - 2])])
                evl = (csem, c(csem))
                for g in range(2):
                    bank = g % 2
                    def tp(e, ht=ht, g=g, bank=bank):
                        psv = psS[bank].ap().rearrange("p (k d) -> p k d", k=4)
                        last = None
                        for k in range(4):
                            ns = g * 4 + k
                            last = e.transpose(psv[:, k, :],
                                               wchs[ht % 2][:, ns * 128:(ns + 1) * 128],
                                               sidf[:, :])
                        return last
                    S.add("tensor", tp, waits=[evl, ("DVE", bank_gate[bank][1]),
                                               ("ACT", bank_gate[bank][0])], inc="PE")
                    evtp_ht[ht] = c("PE")
                    evt = c("PE")
                    S.add("scalar",
                          lambda e, g=g, hh=hh, bank=bank: e.copy(
                              w1stg[:, 0, g * 4:(g + 1) * 4, hh, :],
                              psS[bank].ap().rearrange("p (k d) -> p k d", k=4)),
                          waits=[("PE", evt), ("ST_W", c("ST_W"))], inc="ACT")
                    evc = c("ACT")
                    S.add("vector",
                          lambda e, g=g, hh=hh, bank=bank: e.tensor_sub(
                              w1stg[:, 1, g * 4:(g + 1) * 4, hh, :],
                              psS[bank].ap().rearrange("p (k d) -> p k d", k=4),
                              w1stg[:, 0, g * 4:(g + 1) * 4, hh, :]),
                          waits=[("ACT", evc)], inc="DVE")
                    bank_gate[bank] = (evc, c("DVE"))
            evd = c("DVE")
            for hl in range(2):
                dma("gpsimd", "ST_W", w_in_pv[2 + hl][:, :, htg * 512:(htg + 1) * 512],
                    w1stg[:, hl, :, :, :].rearrange("p a b c -> p a (b c)"),
                    waits=[("DVE", evd)])
        ev_wprep_pe[0] = c("PE")
        ev_wprep_dve[0] = c("DVE")
        ev_wprep_act[0] = c("ACT")
        ev_wprep_stg[0] = c("ST_W")

    ev_cc = {}
    ev_vlast = {}
    ev_vlast_stg = {}
    ev_stores_kv = {}
    ev_oacc_last = {-1: 0}
    ev_pe_flash_last = {-1: 0}

    ev_qmm_flash = [0]

    def emit_flash(L, ev_qact):
        buf = L % 2
        evS, evSC, evST, evLU, evP, evT, evPTE, evPV, evOA = ({} for _ in range(9))
        evLD, evLDV = {}, {}

        def rs(i):
            return i % 8, i // 8

        def load_k(i):
            r, s = rs(i)
            w = [ev_cc[(L, "k", s)]]
            w.append(("PE", ev_vlast[L] if i == 0 else evS[(i - 1, TJ - 1)]))
            dma("sync", "LD_K", kslot.ap().rearrange("p a b -> p (a b)"),
                g_k_v[s][buf][r], waits=w)
            evLD[i] = c("LD_K")

        def load_v(i):
            r, s = rs(i)
            vs = [vslot, vslot2][i % 2]
            w = [ev_cc[(L, "v", s)]]
            if i == 0:
                w.append(("PE", ev_vlast[L]))
            elif i == 1:
                w.append(("PE", ev_qmm_flash[0]))
            else:
                w.append(("PE", evPV[(i - 2, TJ - 1, 3)]))
            vsem = "LD_V%d" % (i % 2)
            dma("sync" if i < 2 else "scalar", vsem,
                vs.ap().rearrange("p a b -> p (a b)"),
                g_v_v[s][buf][r], waits=w)
            evLDV[i] = (vsem, c(vsem))

        def emit_pv(i, j):
            idx = i * TJ + j
            vs = [vslot, vslot2][i % 2]
            for db in range(4):
                def pv(e, j=j, db=db, vs=vs):
                    last = None
                    for kc in range(4):
                        last = e.matmul(psPV[:, db * 512:(db + 1) * 512],
                                        ptsb[:, j % 2, kc, :],
                                        vs[:, kc, db * 512:(db + 1) * 512],
                                        start=(kc == 0), stop=(kc == 3))
                    return last
                w = [("ACT", evPTE[idx]), evLDV[i]]
                prev = (i, j - 1, db) if j >= 1 else (i - 1, TJ - 1, db)
                w.append(("DVE", evOA.get(prev, ev_oacc_last[L - 1] if idx == 0 else 0)))
                S.add("tensor", pv, waits=w, inc="PE")
                evPV[(i, j, db)] = c("PE")
                woa = [("PE", evPV[(i, j, db)])]
                if idx == 0 and db == 0:
                    woa.append(("ST_K", ev_stores_kv[L][0]))
                    woa.append(("ST_K1", ev_stores_kv[L][1]))
                S.add("vector",
                      lambda e, j=j, db=db: e.scalar_tensor_tensor(
                          oacc[:, j, db * 512:(db + 1) * 512],
                          oacc[:, j, db * 512:(db + 1) * 512],
                          alpha[:, j:j + 1],
                          psPV[:, db * 512:(db + 1) * 512], ALU.mult, ALU.add),
                      waits=woa, inc="DVE")
                evOA[(i, j, db)] = c("DVE")

        pe0_flash = c("PE")
        dve0_flash = c("DVE")
        load_k(0)
        load_v(0)
        for i in range(NSUB):
            pnew, pold = i % 2, 1 - (i % 2)

            def emit_s_sc(i, j, pnew, pold):
                idx = i * TJ + j
                bank = idx % 2
                def smm(e, j=j, bank=bank):
                    last = None
                    for dcin in range(DC):
                        last = e.matmul(psS[bank][:, :],
                                        qT[:, dcin, j * 128:(j + 1) * 128],
                                        kslot[:, dcin, :],
                                        start=(dcin == 0), stop=(dcin == DC - 1))
                    return last
                w = [("LD_K", evLD[i])]
                w.append(("ACT", evSC[idx - 2] if idx >= 2 else ev_qact))
                S.add("tensor", smm, waits=w, inc="PE")
                evS[(i, j)] = c("PE")

                wsc = [("PE", evS[(i, j)])]
                wsc.append(("DVE", evST[idx - 4] if idx >= 4 else dve0_flash))
                S.add("scalar",
                      lambda e, j=j, bank=bank: e.copy(sbf[:, j % 4, :], psS[bank][:, :]),
                      waits=wsc, inc="ACT")
                evSC[idx] = c("ACT")

                S.add("vector",
                      lambda e, j=j: e.tensor_reduce(nmraw[:, j:j + 1], sbf[:, j % 4, :],
                                                     AX.X, ALU.max, negate=True),
                      waits=[("ACT", evSC[idx])])
                S.add("vector", lambda e: e.drain())
                S.add("vector",
                      lambda e, j=j, pnew=pnew, pold=pold: e.tensor_scalar(
                          negm[:, pnew, j:j + 1], nmraw[:, j:j + 1],
                          SCALE, negm[:, pold, j:j + 1], ALU.mult, ALU.min))
                S.add("vector", lambda e: e.drain())
                S.add("vector",
                      lambda e, j=j, pnew=pnew, pold=pold: e.tensor_sub(
                          dcol[:, j:j + 1], negm[:, pnew, j:j + 1],
                          negm[:, pold, j:j + 1]),
                      inc="DVE")
                S.add("vector", lambda e: e.drain())
                evST[idx] = c("DVE")

            def emit_ap(i, j, pnew):
                idx = i * TJ + j
                wal = [("DVE", evST[idx])]
                wal.append(("DVE", evOA.get((i - 1, j, 3), ev_oacc_last[L - 1] if i == 0 else 0)))
                S.add("scalar",
                      lambda e, j=j: e.activation(alpha[:, j:j + 1], dcol[:, j:j + 1],
                                                  AF.Exp, bias=0.0, scale=1.0),
                      waits=wal)
                wp = []
                if idx >= 8:
                    wp.append(("PE", evT[idx - 8]))
                    wp.append(("DVE", evLU[idx - 8]))
                else:
                    wp.append(("PE", pe0_flash))
                    wp.append(("ST_V", ev_vlast_stg[L]))
                S.add("scalar",
                      lambda e, j=j, pnew=pnew: e.activation(
                          pex[:, j, :], sbf[:, j % 4, :], AF.Exp,
                          bias=negm[:, pnew, j:j + 1], scale=SCALE,
                          accum_out=lblk[:, j:j + 1]),
                      waits=wp, inc="ACT")
                evP[idx] = c("ACT")

                S.add("vector",
                      lambda e, j=j: e.scalar_tensor_tensor(
                          lrun[:, j:j + 1], lrun[:, j:j + 1], alpha[:, j:j + 1],
                          lblk[:, j:j + 1], ALU.mult, ALU.add),
                      waits=[("ACT", evP[idx])], inc="DVE")
                evLU[idx] = c("DVE")
                S.add("vector", lambda e: e.drain())

            emit_s_sc(i, 0, pnew, pold)
            for j in range(1, TJ):
                emit_s_sc(i, j, pnew, pold)
                emit_ap(i, j - 1, pnew)
            emit_ap(i, TJ - 1, pnew)
            if i + 1 < NSUB:
                load_k(i + 1)
            def emit_t(j):
                idx = i * TJ + j
                half = j % 2
                def tp(e, j=j, half=half):
                    last = None
                    for k in range(4):
                        last = e.transpose(psTs_k[half][:, k, :],
                                           pex[:, j, k * 128:(k + 1) * 128],
                                           sidb[:, :])
                    return last
                wt = [("ACT", evP[idx])]
                if idx >= 2:
                    wt.append(("ACT", evPTE[idx - 2]))
                S.add("tensor", tp, waits=wt, inc="PE")
                evT[idx] = c("PE")
                wpe = [("PE", evT[idx])]
                prev = (i, j - 2, 3) if j >= 2 else (i - 1, TJ - 1, 3)
                if prev in evPV:
                    wpe.append(("PE", evPV[prev]))
                S.add("scalar",
                      lambda e, j=j, half=half: e.copy(
                          ptsb[:, j % 2, :, :].rearrange("p a b -> p (a b)"),
                          psTs[half][:, :]),
                      waits=wpe, inc="ACT")
                evPTE[idx] = c("ACT")

            emit_t(0)
            emit_t(1)
            for j in range(1, TJ):
                emit_pv(i, j - 1)
                if j + 1 < TJ:
                    emit_t(j + 1)
            emit_pv(i, TJ - 1)
            if i + 1 < NSUB:
                load_v(i + 1)
        ev_oacc_last[L] = evOA[(NSUB - 1, TJ - 1, 3)]
        ev_pe_flash_last[L] = evPV[(NSUB - 1, TJ - 1, 3)]
        return c("GPS")

    def emit_tail(L, ev_gps_flash):
        S.add("vector", lambda e: e.reciprocal(rlv[:, :], lrun[:, :]))
        S.add("vector", lambda e: e.drain(), inc="DVE")
        pe0 = c("PE")
        act0 = c("ACT")
        evN = {}
        evTPlast = {-2: pe0, -1: pe0}
        evE = []
        gi = 0
        for j in range(TJ):
            S.add("vector",
                  lambda e, j=j: e.tensor_scalar(attnj[:, j % 2, :], oacc[:, j, :],
                                                 rlv[:, j:j + 1], None, ALU.mult),
                  waits=[("PE", evTPlast[j - 2])], inc="DVE")
            evN[j] = c("DVE")
            for g in range(4):
                half = gi % 2
                def tp(e, j=j, g=g, half=half):
                    last = None
                    for k in range(4):
                        dc = g * 4 + k
                        last = e.transpose(psTs_k[half][:, k, :],
                                           attnj[:, j % 2, dc * 128:(dc + 1) * 128],
                                           sidb[:, :])
                    return last
                w = [("DVE", evN[j])]
                w.append(("ACT", evE[gi - 2] if gi >= 2 else act0))
                S.add("tensor", tp, waits=w, inc="PE")
                evTPlast[j] = c("PE")
                S.add("scalar",
                      lambda e, j=j, g=g, half=half: e.copy(
                          attnT[:, g * 4:(g + 1) * 4, j * 128:(j + 1) * 128],
                          psTs_k[half][:, :, :]),
                      waits=[("PE", evTPlast[j])], inc="ACT")
                evE.append(c("ACT"))
                gi += 1
        ev_norm_all = evN[TJ - 1]
        for s4 in range(4):
            prev_proj_mm[s4] = max(prev_proj_mm[s4], ev_pe_flash_last[L])
        ev_omm, ev_oact = emit_projT(L, woc, sbo, xT, attnT,
                                     extra_first=[("DVE", ev_norm_all)])
        tail_omm[0] = ev_omm
        if L == NL - 1:
            evEv = []
            gi = 0
            for dc in range(DC):
                for g in range(2):
                    half = gi % 2
                    def tp(e, dc=dc, g=g, half=half):
                        last = None
                        for k in range(4):
                            t = g * 4 + k
                            last = e.transpose(psTs_k[half][:, k, :],
                                               xT[:, dc, t * 128:(t + 1) * 128],
                                               sidb[:, :])
                        return last
                    w = [("ACT", ev_oact)]
                    if gi >= 2:
                        w.append(("DVE", evEv[gi - 2]))
                    S.add("tensor", tp, waits=w, inc="PE")
                    evt = c("PE")
                    S.add("vector",
                          lambda e, dc=dc, g=g, half=half: e.tensor_add(
                              x0rm[:, g * 4:(g + 1) * 4, dc * 128:(dc + 1) * 128],
                              x0rm[:, g * 4:(g + 1) * 4, dc * 128:(dc + 1) * 128],
                              psTs_k[half][:, :, :]),
                          waits=[("PE", evt)], inc="DVE")
                    evEv.append(c("DVE"))
                    gi += 1
            S.add("vector", lambda e: e.drain(), inc="DVE")
        return c("DVE"), ev_norm_all

    ev_resid = 0
    ev_norm_prev = [0]
    tail_omm = [0]
    for L in range(NL):
        if L == 0:
            ev_ln1 = emit_ln(lambda t: x0rm[:, t, :],
                             {0: [("LD_X0", 16)], 4: [("LD_X1", 16)]},
                             {0: [("LD_X0", 16), ("LD_INIT", ev_x0)],
                              4: [("LD_X1", 16)]})
            emit_rm_to_ct(ln_row, xT, ev_ln1,
                          extra0=[("LD_INIT", ev_x0)])

        S.add("gpsimd", lambda e: e.memset(negm[:, 1, :], NEG_BIG),
              waits=[("ACT", c("ACT"))])
        S.add("gpsimd", lambda e: e.drain(), inc="GPS")

        if L == 0:
            emit_wprep()
        for s4 in range(4):
            prev_proj_mm[s4] = max(prev_proj_mm[s4], ev_pe_flash_last[L - 1])
        kx = ([("DVE", ev_norm_prev[0])] if L > 0 else
              [("DVE", ev_wprep_dve[0]), ("ACT", ev_wprep_act[0])])
        ev_kmm, ev_kact = emit_projT(L, wkc, sbk, kTst, xT, extra_first=kx)
        buf = L % 2
        evst_k = []
        for s2 in range(2):
            ksem = ("ST_K", "ST_K1")[s2]
            dma("gpsimd", ksem, kv_in_v[s2].rearrange("p (a b) -> p a b", b=512),
                kTst[:, :, s2 * 512:(s2 + 1) * 512],
                waits=[("ACT", ev_kact)])
            evst_k.append(c(ksem))

        def issue_kcc(s2):
            S.add("gpsimd",
                  lambda e, s2=s2, buf=buf: e.collective_compute(
                      "AllGather", ALU.bypass, replica_groups=[list(range(NCORE))],
                      ins=[kv_in[s2 * 128:(s2 + 1) * 128, :].opt()],
                      outs=[g_k[s2][buf].opt()]),
                  waits=[(("ST_K", "ST_K1")[s2], evst_k[s2])], inc="CCK%d" % s2)
            ev_cc[(L, "k", s2)] = ("CCK%d" % s2, c("CCK%d" % s2))
        issue_kcc(0)

        dma("sync", "LD_WV", bvb[:, :], bvn[L:L + 1, :],
            waits=[("PE", ev_pe_flash_last[L - 1])])
        wv_v = wvn[L].rearrange("(a p) d -> p a d", p=128)
        w_wv = [("PE", ev_pe_flash_last[L - 1])]
        dma("sync", "LD_WV", wvlo8[:, :, :], wv_v[:, 0:8, :], waits=w_wv)
        w_wvh = ([("PE", ev_wprep_pe[0]), ("ST_W", ev_wprep_stg[0])] if L == 0
                 else [("PE", tail_omm[0])])
        dma("sync", "LD_WV", wvhi8[:, :, :], wv_v[:, 8:16, :], waits=w_wvh)
        ev_wv = c("LD_WV")
        ev_vact = []
        st_v_hist = []
        for m in range(TJ):
            def vmm(e, m=m):
                last = None
                for dcin in range(DC):
                    src = wvlo8 if dcin < 8 else wvhi8
                    for db in range(4):
                        last = e.matmul(psPV[:, db * 512:(db + 1) * 512],
                                        xT[:, dcin, m * 128:(m + 1) * 128],
                                        src[:, dcin % 8, db * 512:(db + 1) * 512],
                                        start=(dcin == 0), stop=False)
                for db in range(4):
                    last = e.matmul(psPV[:, db * 512:(db + 1) * 512],
                                    sonb[:, :], bvb[0:1, db * 512:(db + 1) * 512],
                                    start=False, stop=True)
                return last
            w = [("LD_WV", ev_wv)]
            if m == 0:
                w.append(("DVE", ev_oacc_last[L - 1]))
                w.append(("ACT", c("ACT")))
            else:
                w.append(("ACT", ev_vact[-1]))
            S.add("tensor", vmm, waits=w, inc="PE")
            ev_vmm = c("PE")
            base = (m % 2) * 4
            for db in range(4):
                wcp = [("PE", ev_vmm)]
                if db == 0 and m >= 2:
                    wcp.append(("ST_V", st_v_hist[m - 2]))
                S.add("scalar",
                      lambda e, db=db, base=base: e.copy(
                          pex[:, base + db, :], psPV[:, db * 512:(db + 1) * 512]),
                      waits=wcp, inc="ACT")
            eva = c("ACT")
            dma("gpsimd", "ST_V",
                kv_in_v[2 + m // 4][:, (m % 4) * 2048:(m % 4 + 1) * 2048],
                pex[:, base:base + 4, :], waits=[("ACT", eva)])
            ev_vact.append(c("ACT"))
            st_v_hist.append(c("ST_V"))
            if m == 3 or m == 7:
                s2 = m // 4
                evst = c("ST_V")
                S.add("gpsimd",
                      lambda e, s2=s2, buf=buf: e.collective_compute(
                          "AllGather", ALU.bypass, replica_groups=[list(range(NCORE))],
                          ins=[kv_in[(2 + s2) * 128:(3 + s2) * 128, :].opt()],
                          outs=[g_v[s2][buf].opt()]),
                      waits=[("ST_V", evst)], inc="CCV%d" % s2)
                ev_cc[(L, "v", s2)] = ("CCV%d" % s2, c("CCV%d" % s2))
                if m == 3:
                    issue_kcc(1)
        ev_vlast[L] = c("PE")
        ev_vlast_stg[L] = c("ST_V")
        ev_stores_kv[L] = (c("ST_K"), c("ST_K1"))
        for s3 in ({0: [0], 1: [1], 2: [2], 3: [3]}.get(L, [])):
            S.add("gpsimd",
                  lambda e, s3=s3: e.collective_compute(
                      "AllGather", ALU.bypass, replica_groups=[list(range(NCORE))],
                      ins=[w_in[s3 * 1024:(s3 + 1) * 1024, :].opt()],
                      outs=[w_alls[s3].opt()]),
                  waits=[("ST_W", ev_wprep_stg[0])], inc="CCW")
        ev_cc["Wall"] = ("CCW", 4)

        ev_qmm, ev_qact = emit_projT(L, wqc, sbq, qT, xT)
        ev_qmm_flash[0] = ev_qmm
        S.add("vector", lambda e: e.memset(oacc[:, :, :], 0.0),
              waits=[("PE", ev_qmm), ("ST_K", c("ST_K")),
                     ("ST_K1", c("ST_K1"))])
        S.add("vector", lambda e: e.drain())
        ev_gps_flash = emit_flash(L, ev_qact)
        ev_resid, ev_na = emit_tail(L, ev_gps_flash)
        ev_norm_prev[0] = ev_na

    ev_ln2 = emit_ln(lambda t: x0rm[:, t, :], [("DVE", ev_resid)],
                     [("PE", c("PE")), ("DVE", ev_resid)])
    emit_rm_to_ct(lambda t: x0rm[:, t, :], xT, ev_resid)
    act0 = c("ACT")
    evX = []
    gi = 0
    for dc in range(DC):
        for g in range(2):
            half = gi % 2
            def tp(e, dc=dc, g=g, half=half):
                last = None
                for k in range(4):
                    t = g * 4 + k
                    last = e.transpose(psTs_k[half][:, k, :],
                                       ln_row(t)[:, dc * 128:(dc + 1) * 128], sidb[:, :])
                return last
            w = [("DVE", ev_ln2), ("ACT", evX[gi - 2] if gi >= 2 else act0)]
            S.add("tensor", tp, waits=w, inc="PE")
            evt = c("PE")
            S.add("scalar",
                  lambda e, dc=dc, g=g, half=half: e.copy(
                      XT[:, dc, g * 512:(g + 1) * 512], psTs[half][:, :]),
                  waits=[("PE", evt)], inc="ACT")
            evX.append(c("ACT"))
            gi += 1
    ev_xtf = c("ACT")

    dve_ffn0 = c("DVE")
    pe_ffn0 = c("PE")
    prev_act_gate = ev_xtf
    evd_hist = []
    evB_all = []
    ev_stg_act_prev = None
    st_out_hist = []
    for half in range(2):
        evA = {}
        evGel = {}
        for ncx in range(NC2):
            r, lc = ncx // 8, ncx % 8
            slot = ncx % 2
            fsem = "LD_F0%d" % slot
            wld = [ev_cc["Wall"]]
            wld.append(("PE", evA[ncx - 2] if ncx >= 2 else c("PE")))
            for hl in range(2):
                dma("sync", fsem, w0buf[:, slot, hl, :], w_all_v[hl][r, lc, :, :],
                    waits=wld if hl == 0 else [])
            evw = (fsem, c(fsem))
            bank = ncx % 2
            def amm(e, slot=slot, bank=bank, half=half):
                last = None
                w0v = w0buf.ap().rearrange("p s l (a b) -> p s l a b", a=DC)
                for dc in range(DC):
                    for hl in range(2):
                        last = e.matmul(psS[bank][:, :],
                                        w0v[:, slot, hl, dc, :],
                                        XT[:, dc, half * 512:(half + 1) * 512],
                                        start=(dc == 0 and hl == 0),
                                        stop=(dc == DC - 1 and hl == 1))
                return last
            w = [evw, ("ACT", prev_act_gate if ncx < 2 else evGel[ncx - 2])]
            S.add("tensor", amm, waits=w, inc="PE")
            evA[ncx] = c("PE")
            dtgt = d16a if ncx < 32 else d16b
            S.add("scalar",
                  lambda e, ncx=ncx, bank=bank, dtgt=dtgt: e.activation(
                      dtgt[:, ncx % 32, :], psS[bank][:, :], AF.Gelu,
                      bias=sb0[:, ncx:ncx + 1], scale=1.0),
                  waits=[("PE", evA[ncx])], inc="ACT")
            evGel[ncx] = c("ACT")
        ev_gelu = c("ACT")
        for pas in range(4):
            for ncx in range(NC2):
                r, lc = ncx // 8, ncx % 8
                g = len(evB_all)
                slot = g % 8
                fsem = "LD_F1%d" % slot
                wld = [("PE", evB_all[g - 8] if g >= 8 else pe_ffn0),
                       ev_cc["Wall"]]
                for hl in range(2):
                    dma("sync", fsem, w1buf[:, slot, hl, :],
                        w_all_v[2 + hl][r, lc, :, pas * 512:(pas + 1) * 512],
                        waits=wld if hl == 0 else [])
                evw = (fsem, c(fsem))
                dsrc = d16a if ncx < 32 else d16b
                def bmm(e, ncx=ncx, slot=slot, dsrc=dsrc):
                    last = None
                    w1v = w1buf.ap().rearrange("p s l (a b) -> p s l a b", a=4)
                    for hl in range(2):
                        for hc in range(4):
                            last = e.matmul(psPV[:, hc * 512:(hc + 1) * 512],
                                            w1v[:, slot, hl, hc, :],
                                            dsrc[:, ncx % 32, :],
                                            start=(ncx == 0 and hl == 0),
                                            stop=(ncx == NC2 - 1 and hl == 1))
                    return last
                w = [evw, ("ACT", ev_gelu)]
                if ncx == 0:
                    if ev_stg_act_prev is not None:
                        w.append(("ACT", ev_stg_act_prev))
                    else:
                        w.append(("DVE", dve_ffn0))
                        w.append(("ACT", c("ACT")))
                S.add("tensor", bmm, waits=w, inc="PE")
                evB_all.append(c("PE"))
            evB = evB_all[-1]
            for hc in range(4):
                hch = pas * 4 + hc
                we = [("PE", evB)]
                if len(evd_hist) >= 4:
                    we.append(("DVE", evd_hist[-4]))
                S.add("scalar",
                      lambda e, hc=hc, hch=hch: e.activation(
                          e16s[:, hc, :], psPV[:, hc * 512:(hc + 1) * 512],
                          AF.Identity, bias=sb1[:, hch:hch + 1], scale=1.0),
                      waits=we, inc="ACT")
                eva = c("ACT")
                wadd = [("ACT", eva)]
                if len(st_out_hist) >= 4:
                    wadd.append(("ST_OUT", st_out_hist[-4]))
                S.add("vector",
                      lambda e, hc=hc, hch=hch, half=half: e.tensor_add(
                          outb[:, hc, :],
                          xT[:, hch, half * 512:(half + 1) * 512],
                          e16s[:, hc, :]),
                      waits=wadd, inc="DVE")
                evd = c("DVE")
                evd_hist.append(evd)
                dma("gpsimd", "ST_OUT",
                    y_t.rearrange("(a p) r -> a p r", p=128)[pas * 4 + hc]
                       [:, half * 512:(half + 1) * 512],
                    outb[:, hc, :], waits=[("DVE", evd)])
                st_out_hist.append(c("ST_OUT"))
            ev_stg_act_prev = c("ACT")
        prev_act_gate = c("ACT")
    ev_out = c("ST_OUT")
    S.add("gpsimd", None, waits=[("ST_OUT", ev_out)])

    from contextlib import ExitStack
    sems = {}
    es = ExitStack()
    with es:
        for sname in ("LD_INIT", "LD_X0", "LD_X1",
                      "LD_PW0", "LD_PW1", "LD_PW2", "LD_PW3",
                      "LD_WV", "LD_WCH0", "LD_WCH1",
                      "LD_K", "LD_V0", "LD_V1", "LD_F00", "LD_F01",
                      "LD_F10", "LD_F11", "LD_F12", "LD_F13",
                      "LD_F14", "LD_F15", "LD_F16", "LD_F17",
                      "ST_K", "ST_K1", "ST_V", "ST_W", "ST_OUT",
                      "CCK0", "CCK1", "CCV0", "CCV1", "CCW",
                      "PE", "ACT", "DVE", "GPS"):
            sems[sname] = es.enter_context(nc.semaphore(sname))
        blk = es.enter_context(nc.Block())
        S.emit(blk, sems)
    n_items = {e: len(S.items[e]) for e in ENGS}
    return nc, n_items


_NC_CACHE = None


def kernel(**inputs):
    global _NC_CACHE
    xs = np.asarray(inputs["L0_attn_x"])
    Wq = np.asarray(inputs["L0_attn_Wq"])
    bq = np.asarray(inputs["L0_attn_bq"])
    Wk = np.asarray(inputs["L0_attn_Wk"])
    bk = np.asarray(inputs["L0_attn_bk"])
    Wv = np.asarray(inputs["L0_attn_Wv"])
    bv = np.asarray(inputs["L0_attn_bv"])
    Wo = np.asarray(inputs["L0_attn_Wo"])
    bo = np.asarray(inputs["L0_attn_bo"])
    W0 = np.asarray(inputs["L0_ffn_W0"], dtype=np.float32)
    bias0 = np.asarray(inputs["L0_ffn_bias0"], dtype=np.float32)
    W1 = np.asarray(inputs["L0_ffn_W1"], dtype=np.float32)
    bias1 = np.asarray(inputs["L0_ffn_bias1"], dtype=np.float32)

    x2 = np.ascontiguousarray(xs.reshape(SEQ, D))

    def colchunk(W):
        return np.ascontiguousarray(
            W.reshape(NL, DC, 128, DC, 128).transpose(0, 3, 2, 1, 4))

    def biasrc(b):
        return np.ascontiguousarray(b.reshape(NL, DC, 128).transpose(2, 0, 1))

    common = {
        "wqc": colchunk(Wq), "wkc": colchunk(Wk), "woc": colchunk(Wo),
        "wvn": np.ascontiguousarray(Wv),
        "bqr": biasrc(bq), "bkr": biasrc(bk), "bor": biasrc(bo),
        "bvn": np.ascontiguousarray(bv),
        "b0r": np.ascontiguousarray(bias0.reshape(NC2, 128).T),
        "b1r": np.ascontiguousarray(bias1.reshape(DC, 128).T),
        "identb": np.eye(128, dtype=ml_dtypes.bfloat16),
        "identf": np.eye(128, dtype=np.float32),
        "onesb": np.ones((1, 128), dtype=ml_dtypes.bfloat16),
    }
    in_maps = []
    for cid in range(NCORE):
        m = dict(common)
        m["x"] = np.ascontiguousarray(x2[cid * RPC:(cid + 1) * RPC])
        m["w0s"] = np.ascontiguousarray(W0[cid * RPC:(cid + 1) * RPC])
        m["w1s"] = np.ascontiguousarray(W1[:, cid * RPC:(cid + 1) * RPC])
        in_maps.append(m)

    if _NC_CACHE is None:
        _NC_CACHE = build_nc()[0]
    res = run_bass_kernel_spmd(_NC_CACHE, in_maps, core_ids=list(range(NCORE)))
    parts = [np.asarray(res.results[cid]["y_t"]).T for cid in range(NCORE)]
    out = np.concatenate(parts, axis=0).reshape(1, SEQ, D)
    return out


if __name__ == "__main__":
    nc, n = build_nc()
    print("built ok; items:", n, "total:", sum(n.values()))
